# version 39 (final)
# baseline (speedup 1.0000x reference)
import numpy as np
from contextlib import ExitStack

import concourse.bass as bass
import concourse.mybir as mybir
from concourse.bass_utils import run_bass_kernel_spmd

F32 = mybir.dt.float32
BF16 = mybir.dt.bfloat16
AF = mybir.ActivationFunctionType
ALU = mybir.AluOpType
AX = mybir.AxisListType

D = 1024
NCH = 8
NH = 16
E = 8
SLAB = 512
TT = 512
LN_EPS = 1e-5

ENGS = ("tensor", "vector", "scalar", "gpsimd", "sync")


class _Sem:
    def __init__(self, handle):
        self.h = handle
        self.n = 0


class _Prog:

    def __init__(self, block):
        self.block = block
        self.ops = {e: [] for e in ENGS}
        self.dirty = {e: set() for e in ENGS}
        self.read = {e: set() for e in ENGS}

    def wait(self, eng, sem, val):
        if val <= 0:
            return
        self.ops[eng].append(lambda E, s=sem, v=val: E.wait_ge(s.h, v))

    def _maybe_drain(self, eng, reads, writes):
        if eng in ("vector", "scalar", "gpsimd"):
            r = {id(h) for h in reads}
            w = {id(h) for h in writes}
            if ((r | w) & self.dirty[eng]) or (w & self.read[eng]):
                self.ops[eng].append(lambda E: E.drain())
                self.dirty[eng].clear()
                self.read[eng].clear()

    def do(self, eng, fn, reads=(), writes=(), inc=None, inc_by=1):
        self._maybe_drain(eng, tuple(reads), tuple(writes))
        if inc is None:
            self.ops[eng].append(lambda E, f=fn: f(E))
        else:
            inc.n += inc_by
            self.ops[eng].append(lambda E, f=fn, s=inc, k=inc_by: f(E).then_inc(s.h, k))
        for h in writes:
            self.dirty[eng].add(id(h))
        for h in reads:
            self.read[eng].add(id(h))
        return inc.n if inc is not None else None

    def flush(self):
        nc_ = self.block.bass
        for eng in ENGS:
            ops = self.ops[eng]
            if not ops:
                continue
            self.ops[eng] = []

            def replay(engine, ops_=ops):
                with nc_.allow_low_precision("bf16 elementwise adds are within tolerance"):
                    for f in ops_:
                        f(engine)

            getattr(self.block, eng)(replay)


class _PsumRole:
    def __init__(self, t):
        self.t = t
        self.last = None

    def wait_writer(self, prog):
        if self.last is not None:
            prog.wait("tensor", *self.last)

    def consumed(self, sem, val):
        self.last = (sem, val)


def _h(ap):
    return ap.tensor if hasattr(ap, "tensor") else ap


def build_program(T):
    assert T % SLAB == 0
    NSLAB = T // SLAB
    NPASS = max(2, T // 1024)
    PW = T // NPASS
    NTT = PW // TT
    assert PW % TT == 0 and NTT >= 1
    XFW = SLAB + 2

    nc = bass.Bass("TRN2", target_bir_lowering=False)

    xs = nc.dram_tensor("xs", [T + 2, D], F32, kind="ExternalInput").ap()
    wp = nc.dram_tensor("wp", [D, D], F32, kind="ExternalInput").ap()
    wqt = nc.dram_tensor("wqt", [D, D], F32, kind="ExternalInput").ap()
    wkt = nc.dram_tensor("wkt", [D, D], F32, kind="ExternalInput").ap()
    wvt = nc.dram_tensor("wvt", [D, D], F32, kind="ExternalInput").ap()
    wot = nc.dram_tensor("wot", [D, D], F32, kind="ExternalInput").ap()
    bp_h = nc.dram_tensor("bp", [128, NCH], F32, kind="ExternalInput").ap()
    bq_h = nc.dram_tensor("bq", [128, NCH], F32, kind="ExternalInput").ap()
    bk_h = nc.dram_tensor("bk", [128, NCH], F32, kind="ExternalInput").ap()
    bv_h = nc.dram_tensor("bv", [128, NCH], F32, kind="ExternalInput").ap()
    bo_h = nc.dram_tensor("bo", [128, NCH], F32, kind="ExternalInput").ap()
    rwt = nc.dram_tensor("rwt", [128, NCH * E], F32, kind="ExternalInput").ap()
    w1 = nc.dram_tensor("w1", [D, E * 1024], F32, kind="ExternalInput").ap()
    w2 = nc.dram_tensor("w2", [E * 1024, D], F32, kind="ExternalInput").ap()
    id_h = nc.dram_tensor("id128", [128, 128], F32, kind="ExternalInput").ap()
    spk_h = nc.dram_tensor("spk", [128, NCH * NH], F32, kind="ExternalInput").ap()
    spkn_h = nc.dram_tensor("spkn", [128, NCH * NH], F32, kind="ExternalInput").ap()
    bpk_h = nc.dram_tensor("bpk", [NH, NCH * 128], F32, kind="ExternalInput").ap()
    sel_h = nc.dram_tensor("selpk", [E, E * 128], F32, kind="ExternalInput").ap()
    ones_h = nc.dram_tensor("onespk", [1, 128], F32, kind="ExternalInput").ap()
    onesv_h = nc.dram_tensor("onesv128", [128, 1], F32, kind="ExternalInput").ap()
    out_h = nc.dram_tensor("out", [128, NCH * T], F32, kind="ExternalOutput").ap()
    r1n_hbm = nc.dram_tensor("r1n_hbm", [128, NCH * T], BF16, kind="Internal").ap()

    st = ExitStack()
    sbt = lambda name, shape, dt: st.enter_context(nc.sbuf_tensor(name, shape, dt)).ap()
    semh = lambda name: _Sem(st.enter_context(nc.semaphore(name)))

    s_ld = semh("s_ld")
    s_pe = semh("s_pe")
    s_dve = semh("s_dve")
    s_act = semh("s_act")
    s_hw = semh("s_hw")

    block = st.enter_context(nc.Block())
    p = _Prog(block)

    rwt_sb = sbt("rwt_sb", [128, NCH * E], BF16)
    bo_sb = sbt("bo_sb", [128, NCH], F32)
    idb = sbt("idb", [128, 128], BF16)
    id32 = sbt("id32", [128, 128], F32)
    onescol = sbt("onescol", [1, 128], BF16)
    onesv = sbt("onesv", [128, 1], BF16)
    gates = sbt("gates", [E, T], BF16)
    cst_eps = sbt("cst_eps", [128, 1], F32)
    nc.const_aps.register(F32, LN_EPS, cst_eps)
    cst_one = sbt("cst_one", [128, 1], F32)
    nc.const_aps.register(F32, 1.0, cst_one)

    def dma(eng, out_ap, in_ap, sem_, writes=()):
        return p.do(eng, lambda En, o=out_ap, i_=in_ap: En.dma_start(out=o, in_=i_),
                    writes=writes, inc=sem_, inc_by=16)

    def ldb():
        p.wait("gpsimd", s_ld, s_ld.n)

    p.do("vector", lambda En: En.memset(cst_eps, LN_EPS), writes=[_h(cst_eps)], inc=s_dve)
    p.do("vector", lambda En: En.memset(cst_one, 1.0), writes=[_h(cst_one)], inc=s_dve)
    dma("gpsimd", idb[:], id_h[:], s_ld, writes=[_h(idb)])
    dma("gpsimd", id32[:], id_h[:], s_ld, writes=[_h(id32)])
    dma("gpsimd", onescol[:], ones_h[:], s_ld, writes=[_h(onescol)])
    dma("gpsimd", onesv[:], onesv_h[:], s_ld, writes=[_h(onesv)])
    dma("gpsimd", rwt_sb[:], rwt[:], s_ld, writes=[_h(rwt_sb)])
    dma("gpsimd", bo_sb[:], bo_h[:], s_ld, writes=[_h(bo_sb)])

    pa = ExitStack()
    asb = lambda name, shape, dt: pa.enter_context(nc.sbuf_tensor(name, shape, dt)).ap()
    cqt = asb("cqt", [128, NCH * 1024], BF16)
    ckt = asb("ckt", [128, NCH * 1024], BF16)
    cvt = asb("cvt", [128, NCH * 1024], BF16)
    wot_sb = asb("wot_sb", [128, NCH * 1024], BF16)
    cbq = asb("cbq", [128, NCH], F32)
    cbk = asb("cbk", [128, NCH], F32)
    cbv = asb("cbv", [128, NCH], F32)
    spk = asb("spk_sb", [128, NCH * NH], BF16)
    spkn = asb("spkn_sb", [128, NCH * NH], BF16)
    bpk = asb("bpk_sb", [NH, NCH * 128], BF16)

    dma("gpsimd", spk[:], spk_h[:], s_ld, writes=[_h(spk)])
    dma("gpsimd", spkn[:], spkn_h[:], s_ld, writes=[_h(spkn)])
    dma("gpsimd", bpk[:], bpk_h[:], s_ld, writes=[_h(bpk)])

    ph = ExitStack()
    psb = lambda name, shape, dt: ph.enter_context(nc.sbuf_tensor(name, shape, dt)).ap()
    pst = lambda name, shape, dt: ph.enter_context(nc.psum_tensor(name, shape, dt)).ap()
    wp_sb = psb("wp_sb", [128, NCH * 1024], BF16)
    wxt = psb("wxt_sb", [128, 2 * NCH * 1024], BF16)
    biasf = psb("bias_f", [128, 3 * NCH], F32)
    bpb = psb("bpb", [128, NCH], BF16)
    psf = [_PsumRole(pst("ps_f0", [128, 512], F32)),
           _PsumRole(pst("ps_f1", [128, 512], F32))]
    psfb = _PsumRole(pst("ps_fb", [128, 8], F32))

    dma("gpsimd", wp_sb.rearrange("p (c n) -> p c n", n=1024),
        wp.rearrange("(c p) n -> p c n", p=128), s_ld, writes=[_h(wp_sb)])
    dma("gpsimd", bpb, bp_h[:], s_ld, writes=[_h(bpb)])
    for i, bh in enumerate([bq_h, bk_h, bv_h]):
        dma("gpsimd", biasf[:, i * NCH:(i + 1) * NCH], bh[:], s_ld, writes=[_h(biasf)])
    base_ld = s_ld.n

    proj_done = {}
    for xi, (wxt_h, c_sb, cb_sb, scale) in enumerate(
            [(wqt, cqt, cbq, 0.125), (wkt, ckt, cbk, 1.0), (wvt, cvt, cbv, 1.0)]):
        par0 = (xi % 2) * NCH * 1024
        ldb()
        if xi >= 2:
            p.wait("gpsimd", s_pe, proj_done[xi - 2])
        v_wx = dma("gpsimd",
                   wxt[:, par0: par0 + NCH * 1024].rearrange("p (c n) -> p c n", n=1024),
                   wxt_h.rearrange("(c p) n -> p c n", p=128), s_ld, writes=[_h(wxt)])
        for dc in range(NCH):
            for jg in range(2):
                role = psf[(dc * 2 + jg) % 2]
                role.wait_writer(p)
                p.wait("tensor", s_ld, v_wx)
                for kc in range(NCH):
                    lhs = wp_sb[:, kc * 1024 + dc * 128: kc * 1024 + dc * 128 + 128]
                    rhs = wxt[:, par0 + kc * 1024 + jg * 512: par0 + kc * 1024 + jg * 512 + 512]
                    v = p.do("tensor",
                             lambda En, o=role.t, l=lhs, r=rhs, a=(kc == 0), b=(kc == NCH - 1):
                             En.matmul(o[:], l, r, start=a, stop=b),
                             inc=(s_pe if kc == NCH - 1 else None))
                p.wait("scalar", s_pe, v)
                dst = c_sb[:, dc * 1024 + jg * 512: dc * 1024 + jg * 512 + 512]
                va = p.do("scalar",
                          lambda En, o=dst, i_=role.t, sc=scale:
                          En.activation(o, i_[:], AF.Copy, scale=sc),
                          reads=[_h(role.t)], writes=[_h(c_sb)], inc=s_act)
                role.consumed(s_act, va)
        for jc in range(NCH):
            psfb.wait_writer(p)
            p.wait("tensor", s_ld, base_ld)
            for kc in range(NCH):
                lhs = wxt[:, par0 + kc * 1024 + jc * 128: par0 + kc * 1024 + jc * 128 + 128]
                v = p.do("tensor",
                         lambda En, o=psfb.t, l=lhs, r=bpb[:, kc:kc + 1],
                         a=(kc == 0), b=(kc == NCH - 1):
                         En.matmul(o[:, 0:1], l, r, start=a, stop=b),
                         inc=(s_pe if kc == NCH - 1 else None))
            p.wait("vector", s_pe, v)
            bcol = biasf[:, xi * NCH + jc: xi * NCH + jc + 1]
            va = p.do("vector",
                      lambda En, o=cb_sb[:, jc:jc + 1], i_=psfb.t, bb=bcol, sc=scale:
                      En.tensor_scalar(o, i_[:, 0:1], bb, sc, ALU.add, ALU.mult),
                      reads=[_h(psfb.t)], writes=[_h(cb_sb)], inc=s_dve)
            psfb.consumed(s_dve, va)
        proj_done[xi] = s_pe.n

    ldb()
    dma("gpsimd", wot_sb.rearrange("p (c n) -> p c n", n=1024),
        wot.rearrange("(c p) n -> p c n", p=128), s_ld, writes=[_h(wot_sb)])
    v_wot_ld = s_ld.n
    p.wait("gpsimd", s_pe, s_pe.n)
    p.wait("gpsimd", s_dve, s_dve.n)
    p.wait("gpsimd", s_act, s_act.n)
    p.flush()
    ph.close()

    p.wait("gpsimd", s_pe, s_pe.n)
    p.wait("gpsimd", s_dve, s_dve.n)
    p.wait("gpsimd", s_act, s_act.n)
    p.flush()
    ph.close()
    ph = ExitStack()
    psb = lambda name, shape, dt: ph.enter_context(nc.sbuf_tensor(name, shape, dt)).ap()
    pst = lambda name, shape, dt: ph.enter_context(nc.psum_tensor(name, shape, dt)).ap()
    xbt = psb("xbt", [128, 2 * 1024], BF16)
    xh2 = psb("xh2", [34, 1024], BF16)
    x_fm = psb("x_fm", [128, 2 * NCH * XFW], BF16)
    q_sb = psb("q_sb", [128, 2 * NCH * XFW], BF16)
    k_sb = psb("k_sb", [128, 2 * NCH * XFW], BF16)
    v_sb = psb("v_sb", [128, 2 * NCH * XFW], BF16)
    pscr = psb("pscr", [128, 2 * 512], BF16)
    pscr2 = psb("pscr2", [128, 2 * 512], BF16)
    smt = psb("smt", [16, 4 * 512], BF16)
    wpl = psb("wpl", [16, 3 * 512], BF16)
    sqscr = psb("sqscr", [128, 2 * 512], BF16)
    wbe = psb("wbe", [128, 6 * 512], BF16)
    stn = psb("stn", [1, 3 * 512], F32)
    stb = psb("stb", [1, 1024], BF16)
    r1n_stg = psb("r1n_stg", [128, 2 * NCH * SLAB], BF16)
    ps_tb = [_PsumRole(pst("ps_tb0", [128, 1024], BF16)),
             _PsumRole(pst("ps_tb1", [128, 1024], BF16))]
    ps_mm = [_PsumRole(pst("ps_mm0", [128, 512], F32)),
             _PsumRole(pst("ps_mm1", [128, 512], F32))]
    ps_s = [_PsumRole(pst("ps_s0", [128, 512], F32)),
            _PsumRole(pst("ps_s1", [128, 512], F32)),
            _PsumRole(pst("ps_s2", [128, 512], F32))]
    ps_st = _PsumRole(pst("ps_st", [128, 512], F32))

    PB = NCH * XFW
    xfc = lambda s, dc: (s % 2) * PB + dc * XFW
    r1c = lambda s, dc: q_sb[:, (s % 2) * PB + dc * XFW: (s % 2) * PB + dc * XFW + 512]
    mixc = lambda s, dc: k_sb[:, (s % 2) * PB + dc * XFW: (s % 2) * PB + dc * XFW + 512]
    sm = lambda i: smt[:, i * 512:(i + 1) * 512]
    wp_ = lambda i: wpl[:, i * 512:(i + 1) * 512]
    sn = lambda i: stn[:, i * 512:(i + 1) * 512]

    xtile_free = [None, None]
    xh_free = [None, None]
    g_qkvmm = [0, 0]
    g_r1 = [0, 0]
    g_wo = [0, 0]
    g_wsum = [0, 0]
    g_r1n = [0, 0]
    g_exp = 0
    g_mb = 0
    g_lnact = 0
    rstg_free = [None, None]
    pslots = [pscr[:, 0:512], pscr[:, 512:1024], pscr2[:, 0:512],
              pscr2[:, 512:1024], sqscr[:, 0:512], sqscr[:, 512:1024]]
    pslot_t = [_h(pscr), _h(pscr), _h(pscr2), _h(pscr2), _h(sqscr), _h(sqscr)]
    pslot_guard = [None] * 6
    wbe_guard = [None] * 6
    SS = {}

    def seg_xt(s):
        par = s % 2
        r0 = s * SLAB
        p.wait("scalar", s_dve, g_r1[par])
        p.wait("scalar", s_pe, g_qkvmm[par])
        for tt in range(4):
            tb = xbt[:, (tt % 2) * 1024:(tt % 2) * 1024 + 1024]
            ldb()
            if xtile_free[tt % 2] is not None:
                p.wait("gpsimd", *xtile_free[tt % 2])
            v_ld = dma("gpsimd", tb, xs[r0 + tt * 128: r0 + (tt + 1) * 128, :], s_ld,
                       writes=[_h(xbt)])
            role = ps_tb[tt % 2]
            role.wait_writer(p)
            p.wait("tensor", s_ld, v_ld)
            for dc in range(NCH):
                v = p.do("tensor",
                         lambda En, o=role.t, d_=dc, src=tb[:, dc * 128:(dc + 1) * 128]:
                         En.transpose(o[:, d_ * 128:(d_ + 1) * 128], src, idb[:]),
                         inc=(s_pe if dc == NCH - 1 else None))
            xtile_free[tt % 2] = (s_pe, v)
            p.wait("scalar", s_pe, v)
            dst = x_fm[:, par * PB: (par + 1) * PB].rearrange(
                "p (c n) -> p c n", n=XFW)[:, :, tt * 128:(tt + 1) * 128]
            va = p.do("scalar",
                      lambda En, o=dst, i_=role.t:
                      En.copy(o, i_[:].rearrange("p (c n) -> p c n", n=128)),
                      reads=[_h(role.t)], writes=[_h(x_fm)], inc=s_act)
            role.consumed(s_act, va)
        ldb()
        if xh_free[par] is not None:
            p.wait("gpsimd", *xh_free[par])
        v_ld = dma("gpsimd", xh2[par * 32: par * 32 + 2, :], xs[r0 + 512: r0 + 514, :],
                   s_ld, writes=[_h(xh2)])
        role = ps_tb[0]
        role.wait_writer(p)
        p.wait("tensor", s_ld, v_ld)
        for dc in range(NCH):
            v = p.do("tensor",
                     lambda En, o=role.t, d_=dc,
                     src=xh2[par * 32: par * 32 + 2, dc * 128:(dc + 1) * 128],
                     ii=idb[par * 32: par * 32 + 2, par * 32: par * 32 + 2]:
                     En.transpose(o[:, d_ * 2: d_ * 2 + 2], src, ii),
                     inc=(s_pe if dc == NCH - 1 else None))
        xh_free[par] = (s_pe, v)
        p.wait("scalar", s_pe, v)
        dst = x_fm[:, par * PB: (par + 1) * PB].rearrange(
            "p (c n) -> p c n", n=XFW)[:, :, 512:514]
        va = p.do("scalar",
                  lambda En, o=dst, i_=role.t:
                  En.copy(o, i_[:, 0:2 * NCH].rearrange("p (c n) -> p c n", n=2)),
                  reads=[_h(role.t)], writes=[_h(x_fm)], inc=s_act)
        role.consumed(s_act, va)
        SS[("xfm", s)] = s_act.n

    def seg_qkv(s):
        par = s % 2
        p.wait("tensor", s_act, SS[("xfm", s)])
        p.wait("tensor", s_ld, v_wot_ld)
        p.wait("scalar", s_pe, g_wo[par])
        p.wait("scalar", s_dve, g_wsum[par])
        v_evac = {}
        gi = 0
        for ti, c_sb, cb_sb, dst_sb in ((1, ckt, cbk, k_sb), (2, cvt, cbv, v_sb),
                                        (0, cqt, cbq, q_sb)):
            if ti == 0:
                p.wait("scalar", s_dve, g_r1n[par])
            for dc in range(NCH):
                for gg in range(2):
                    g0, gw = (0, 257) if gg == 0 else (257, 257)
                    role = ps_mm[gi % 2]
                    gi += 1
                    role.wait_writer(p)
                    for kc in range(NCH):
                        lhs = c_sb[:, kc * 1024 + dc * 128: kc * 1024 + dc * 128 + 128]
                        rhs = x_fm[:, par * PB + kc * XFW + g0: par * PB + kc * XFW + g0 + gw]
                        v = p.do("tensor",
                                 lambda En, o=role.t, l=lhs, r=rhs, w_=gw,
                                 a=(kc == 0), b=(kc == NCH - 1):
                                 En.matmul(o[:, 0:w_], l, r, start=a, stop=b),
                                 inc=(s_pe if kc == NCH - 1 else None))
                    p.wait("scalar", s_pe, v)
                    dst = dst_sb[:, par * PB + dc * XFW + g0: par * PB + dc * XFW + g0 + gw]
                    va = p.do("scalar",
                              lambda En, o=dst, i_=role.t, w_=gw, bb=cb_sb[:, dc:dc + 1]:
                              En.activation(o, i_[:, 0:w_], AF.Identity, bias=bb),
                              reads=[_h(role.t)], writes=[_h(dst_sb)], inc=s_act)
                    role.consumed(s_act, va)
                    v_evac[(ti, dc)] = va
        g_qkvmm[par] = v
        SS[("evac", s)] = v_evac

    def seg_scores(s):
        par = s % 2
        v_evac = SS[("evac", s)]
        ps_s[0].wait_writer(p)
        ps_s[2].wait_writer(p)
        si = 0
        vmm = 0
        for dc in range(NCH):
            prods = {}
            for o in (0, 1, 2):
                slot = si % 6
                si += 1
                scr = pslots[slot]
                if pslot_guard[slot] is not None:
                    p.wait("vector", *pslot_guard[slot])
                p.wait("vector", s_act, max(v_evac[(0, dc)], v_evac[(1, dc)]))
                vv = p.do("vector",
                          lambda En, o_=scr,
                          a=q_sb[:, par * PB + dc * XFW + 1: par * PB + dc * XFW + 513],
                          b=k_sb[:, par * PB + dc * XFW + o: par * PB + dc * XFW + o + 512]:
                          En.tensor_mul(o_, a, b),
                          reads=[_h(q_sb), _h(k_sb)], writes=[pslot_t[slot]], inc=s_dve)
                prods[o] = (slot, scr, vv)
            for bank, terms in ((ps_s[0], ((0, spk), (1, spkn))),
                                (ps_s[2], ((2, spk), (1, spkn)))):
                for j, (o, stat) in enumerate(terms):
                    slot, scr, vv = prods[o]
                    p.wait("tensor", s_dve, vv)
                    vmm = p.do("tensor",
                               lambda En, ot=bank.t, l=stat[:, dc * NH:(dc + 1) * NH], r=scr,
                               a=(dc == 0 and j == 0), b=(dc == NCH - 1 and j == 1):
                               En.matmul(ot[0:NH, :], l, r, start=a, stop=b),
                               inc=s_pe)
                    pslot_guard[slot] = (s_pe, vmm)
        SS[("scores", s)] = vmm

    def seg_softmax_a(s):
        p.wait("scalar", s_pe, SS[("scores", s)])
        p.wait("scalar", s_act, g_exp)
        p.wait("scalar", s_pe, g_mb)
        p.do("scalar", lambda En: En.activation(sm(0), ps_s[0].t[0:NH, :], AF.Exp),
             reads=[_h(ps_s[0].t)], writes=[_h(smt)], inc=s_act)
        v_e = p.do("scalar", lambda En: En.activation(sm(1), ps_s[2].t[0:NH, :], AF.Exp),
                   reads=[_h(ps_s[2].t)], writes=[_h(smt)], inc=s_act)
        p.wait("vector", s_act, v_e)
        v_sm = p.do("vector", lambda En: En.scalar_tensor_tensor(sm(2), sm(0), 1.0, sm(1), ALU.add, ALU.add),
                    reads=[_h(smt)], writes=[_h(smt)], inc=s_dve)
        SS[("v_sm", s)] = v_sm
        return v_e

    def seg_softmax_b(s):
        p.wait("scalar", s_dve, SS[("v_sm", s)])
        p.do("scalar", lambda En: En.activation(sm(3), sm(2), AF.Ln),
             reads=[_h(smt)], writes=[_h(smt)], inc=s_act)
        v_rr = p.do("scalar", lambda En: En.activation(sm(2), sm(3), AF.Exp, scale=-1.0),
                    reads=[_h(smt)], writes=[_h(smt)], inc=s_act)
        p.wait("vector", s_act, v_rr)
        p.do("vector", lambda En: En.tensor_mul(wp_(0), sm(0), sm(2)),
             reads=[_h(smt)], writes=[_h(wpl)], inc=s_dve)
        v_w = p.do("vector", lambda En: En.tensor_mul(wp_(2), sm(1), sm(2)),
                   reads=[_h(smt)], writes=[_h(wpl)], inc=s_dve)
        ps_s[0].consumed(s_dve, v_w)
        ps_s[2].consumed(s_dve, v_w)
        SS[("v_w", s)] = v_w

    def seg_wsum(s):
        par = s % 2
        p.wait("tensor", s_dve, SS[("v_w", s)])
        v_evac = SS[("evac", s)]
        wbsets = [(ps_s[0], ps_s[1], ps_s[2]), (ps_mm[0], ps_mm[1], ps_st)]
        v_mixc = {}
        for dc in range(NCH):
            setr = wbsets[dc % 2]
            v_mm = {}
            for o in range(3):
                setr[o].wait_writer(p)
                rsrc = sm(2) if o == 1 else wp_(o)
                v_mm[o] = p.do("tensor",
                               lambda En, ot=setr[o].t, l=bpk[:, dc * 128:(dc + 1) * 128], r=rsrc:
                               En.matmul(ot[:], l, r, start=True, stop=True),
                               inc=s_pe)
            vb = {}
            for o in range(3):
                sl = (dc % 2) * 3 + o
                if wbe_guard[sl] is not None:
                    p.wait("scalar", *wbe_guard[sl])
                p.wait("scalar", s_pe, v_mm[o])
                vb[o] = p.do("scalar",
                             lambda En, o_=wbe[:, sl * 512:(sl + 1) * 512], i_=setr[o].t:
                             En.copy(o_, i_[:]),
                             reads=[_h(setr[o].t)], writes=[_h(wbe)], inc=s_act)
                setr[o].consumed(s_act, vb[o])
            p.wait("vector", s_act, v_evac[(2, dc)])
            s0i = (dc % 2) * 2
            s1i = s0i + 1
            t0 = pslots[s0i]
            sc0 = pslots[s1i]
            for sl in (s0i, s1i):
                if pslot_guard[sl] is not None:
                    p.wait("vector", *pslot_guard[sl])
                    pslot_guard[sl] = None
            vbase = par * PB + dc * XFW
            e_ = lambda o: wbe[:, ((dc % 2) * 3 + o) * 512: ((dc % 2) * 3 + o) * 512 + 512]
            p.wait("vector", s_act, vb[1])
            p.do("vector", lambda En, o_=t0, w_=e_(0), vv=v_sb[:, vbase: vbase + 512]:
                 En.tensor_mul(o_, w_, vv),
                 reads=[_h(wbe), _h(v_sb)], writes=[pslot_t[s0i]], inc=s_dve)
            p.do("vector", lambda En, o_=sc0, w_=e_(1), vv=v_sb[:, vbase + 1: vbase + 513]:
                 En.tensor_mul(o_, w_, vv),
                 reads=[_h(wbe), _h(v_sb)], writes=[pslot_t[s1i]], inc=s_dve)
            p.do("vector", lambda En, o_=t0, a=t0, b=sc0: En.tensor_add(o_, a, b),
                 reads=[pslot_t[s0i], pslot_t[s1i]], writes=[pslot_t[s0i]], inc=s_dve)
            p.wait("vector", s_act, vb[2])
            p.do("vector", lambda En, o_=sc0, w_=e_(2), vv=v_sb[:, vbase + 2: vbase + 514]:
                 En.tensor_mul(o_, w_, vv),
                 reads=[_h(wbe), _h(v_sb)], writes=[pslot_t[s1i]], inc=s_dve)
            va = p.do("vector", lambda En, o_=mixc(s, dc), a=t0, b=sc0:
                      En.tensor_add(o_, a, b),
                      reads=[pslot_t[s0i], pslot_t[s1i]], writes=[_h(k_sb)], inc=s_dve)
            for o in range(3):
                wbe_guard[(dc % 2) * 3 + o] = (s_dve, va)
            v_mixc[dc] = va
        g_wsum[par] = va
        SS[("mixc", s)] = v_mixc

    def seg_wo(s):
        par = s % 2
        v_mixc = SS[("mixc", s)]
        for dc in range(NCH):
            role = ps_mm[dc % 2]
            role.wait_writer(p)
            for kc in range(NCH):
                p.wait("tensor", s_dve, v_mixc[kc])
                lhs = wot_sb[:, kc * 1024 + dc * 128: kc * 1024 + dc * 128 + 128]
                v = p.do("tensor",
                         lambda En, o=role.t, l=lhs, r=mixc(s, kc),
                         a=(kc == 0), b=(kc == NCH - 1):
                         En.matmul(o[:], l, r, start=a, stop=b),
                         inc=(s_pe if kc == NCH - 1 else None))
            g_wo[par] = v
            p.wait("vector", s_pe, v)
            va = p.do("vector",
                      lambda En, o=r1c(s, dc), i_=role.t,
                      bb=bo_sb[:, dc:dc + 1],
                      xres=x_fm[:, par * PB + dc * XFW + 1: par * PB + dc * XFW + 513]:
                      En.scalar_tensor_tensor(o, i_[:], bb, xres, ALU.add, ALU.add),
                      reads=[_h(role.t), _h(x_fm)], writes=[_h(q_sb)], inc=s_dve)
            role.consumed(s_dve, va)
        g_r1[par] = s_dve.n

    def seg_ln1_stats_mm(s):
        p.wait("tensor", s_dve, g_r1[s % 2])
        ps_s[1].wait_writer(p)
        ps_st.wait_writer(p)
        v2s = 0
        for dc in range(NCH):
            p.do("tensor",
                 lambda En, o=ps_s[1].t, r=r1c(s, dc), a=(dc == 0), b=(dc == NCH - 1):
                 En.matmul(o[0:1, :], onesv[:], r, start=a, stop=b),
                 inc=s_pe)
            slot = 4 + dc % 2
            sq = pslots[slot]
            if pslot_guard[slot] is not None:
                p.wait("scalar", *pslot_guard[slot])
                pslot_guard[slot] = None
            if dc == 0:
                p.wait("scalar", s_dve, g_r1[s % 2])
            vv = p.do("scalar",
                      lambda En, o=sq, r=r1c(s, dc): En.activation(o, r, AF.Square),
                      reads=[_h(q_sb)], writes=[pslot_t[slot]], inc=s_act)
            p.wait("tensor", s_act, vv)
            v2s = p.do("tensor",
                       lambda En, o=ps_st.t, r=sq, a=(dc == 0), b=(dc == NCH - 1):
                       En.matmul(o[0:1, :], onesv[:], r, start=a, stop=b),
                       inc=s_pe)
            pslot_guard[slot] = (s_pe, v2s)
        SS[("v2s", s)] = v2s

    def seg_ln1_stats_tail(s):
        p.wait("vector", s_pe, SS[("v2s", s)])
        p.wait("vector", s_act, g_lnact)
        p.wait("vector", s_pe, SS.get("v_mb", 0))
        p.do("vector", lambda En: En.tensor_scalar_mul(sn(0), ps_s[1].t[0:1, :], 1.0 / D),
             reads=[_h(ps_s[1].t)], writes=[_h(stn)], inc=s_dve)
        p.do("vector", lambda En: En.tensor_scalar_mul(sn(1), ps_st.t[0:1, :], 1.0 / D),
             reads=[_h(ps_st.t)], writes=[_h(stn)], inc=s_dve)
        p.do("vector", lambda En: En.tensor_mul(sn(2), sn(0), sn(0)),
             reads=[_h(stn)], writes=[_h(stn)], inc=s_dve)
        v_var = p.do("vector", lambda En: En.tensor_sub(sn(1), sn(1), sn(2)),
                     reads=[_h(stn)], writes=[_h(stn)], inc=s_dve)
        ps_s[1].consumed(s_dve, v_var)
        ps_st.consumed(s_dve, v_var)
        p.wait("scalar", s_dve, v_var)
        p.do("scalar", lambda En: En.activation(sn(2), sn(1), AF.Ln, bias=LN_EPS),
             reads=[_h(stn)], writes=[_h(stn)], inc=s_act)
        v_rs = p.do("scalar", lambda En: En.activation(sn(1), sn(2), AF.Exp, scale=-0.5),
                    reads=[_h(stn)], writes=[_h(stn)], inc=s_act)
        SS[("v_rs", s)] = v_rs
        return v_rs

    def seg_ln1_stats_tail_b(s):
        p.wait("vector", s_act, SS[("v_rs", s)])
        p.wait("vector", s_pe, SS.get("v_mb", 0))
        p.do("vector", lambda En: En.tensor_mul(sn(2), sn(0), sn(1)),
             reads=[_h(stn)], writes=[_h(stn)], inc=s_dve)
        p.do("vector", lambda En: En.tensor_copy(stb[:, 0:512], sn(1)),
             reads=[_h(stn)], writes=[_h(stb)], inc=s_dve)
        v_stb = p.do("vector", lambda En: En.tensor_copy(stb[:, 512:1024], sn(2)),
                     reads=[_h(stn)], writes=[_h(stb)], inc=s_dve)
        SS[("v_stb", s)] = v_stb

    def seg_ln1_tail(s):
        par = s % 2
        p.wait("tensor", s_dve, SS[("v_stb", s)])
        ps_s[1].wait_writer(p)
        p.do("tensor", lambda En: En.matmul(ps_s[1].t[:], onescol[0:1, :],
                                            stb[0:1, 0:512], start=True, stop=True),
             inc=s_pe)
        ps_st.wait_writer(p)
        v_mb = p.do("tensor", lambda En: En.matmul(ps_st.t[:], onescol[0:1, :],
                                                   stb[0:1, 512:1024], start=True, stop=True),
                    inc=s_pe)
        SS["v_mb"] = v_mb
        p.wait("scalar", s_pe, v_mb)
        sbp = pslots[2]
        mbp = pslots[3]
        for sl in (2, 3):
            if pslot_guard[sl] is not None:
                p.wait("scalar", *pslot_guard[sl])
                pslot_guard[sl] = None
        p.do("scalar", lambda En: En.copy(sbp, ps_s[1].t[:]),
             reads=[_h(ps_s[1].t)], writes=[pslot_t[2]], inc=s_act)
        v_bb = p.do("scalar", lambda En: En.copy(mbp, ps_st.t[:]),
                    reads=[_h(ps_st.t)], writes=[pslot_t[3]], inc=s_act)
        ps_s[1].consumed(s_act, v_bb)
        ps_st.consumed(s_act, v_bb)
        pslot_guard[2] = (s_act, v_bb)
        pslot_guard[3] = (s_act, v_bb)
        p.wait("vector", s_act, v_bb)
        if rstg_free[par] is not None:
            p.wait("vector", *rstg_free[par])
        for dc in range(NCH):
            t0n = pslots[dc % 2]
            p.do("vector", lambda En, o=t0n, r=r1c(s, dc):
                 En.tensor_mul(o, r, sbp),
                 reads=[_h(q_sb), pslot_t[2]], writes=[pslot_t[dc % 2]], inc=s_dve)
            p.do("vector",
                 lambda En, o=r1n_stg[:, par * NCH * SLAB + dc * SLAB: par * NCH * SLAB + (dc + 1) * SLAB],
                 a=t0n:
                 En.tensor_sub(o, a, mbp),
                 reads=[pslot_t[dc % 2], pslot_t[3]], writes=[_h(r1n_stg)], inc=s_dve)
        g_r1n[par] = s_dve.n
        p.wait("sync", s_hw, s_hw.n)
        p.wait("sync", s_dve, s_dve.n)
        p.do("sync",
             lambda En, o=r1n_hbm.rearrange("p (c t) -> p c t", t=T)[:, :, s * SLAB:(s + 1) * SLAB],
             i_=r1n_stg[:, par * NCH * SLAB:(par + 1) * NCH * SLAB].rearrange("p (c t) -> p c t", t=SLAB):
             En.dma_start(out=o, in_=i_),
             inc=s_hw, inc_by=16)
        rstg_free[par] = (s_hw, s_hw.n)

    seg_xt(0)
    seg_qkv(0)
    seg_xt(1)
    for s in range(NSLAB):
        seg_scores(s)
        if s >= 1:
            seg_ln1_stats_tail_b(s - 1)
        v_e = seg_softmax_a(s)
        g_exp = v_e
        if s >= 1:
            seg_ln1_tail(s - 1)
        seg_softmax_b(s)
        if s + 1 < NSLAB:
            seg_qkv(s + 1)
        seg_wsum(s)
        seg_wo(s)
        seg_ln1_stats_mm(s)
        if s + 2 < NSLAB:
            seg_xt(s + 2)
        v_rs = seg_ln1_stats_tail(s)
        g_lnact = v_rs
        g_mb = s_pe.n
    seg_ln1_stats_tail_b(NSLAB - 1)
    seg_ln1_tail(NSLAB - 1)

    p.wait("gpsimd", s_pe, s_pe.n)
    p.wait("gpsimd", s_dve, s_dve.n)
    p.wait("gpsimd", s_act, s_act.n)
    p.wait("gpsimd", s_hw, s_hw.n)
    for eng in ("tensor", "vector", "scalar", "sync"):
        p.wait(eng, s_pe, s_pe.n)
        p.wait(eng, s_dve, s_dve.n)
        p.wait(eng, s_act, s_act.n)
    p.flush()
    ph.close()
    pa.close()

    ph = ExitStack()
    psb = lambda name, shape, dt: ph.enter_context(nc.sbuf_tensor(name, shape, dt)).ap()
    pst = lambda name, shape, dt: ph.enter_context(nc.psum_tensor(name, shape, dt)).ap()
    r1n_ps = psb("r1n_ps", [128, 2 * NCH * PW], BF16)
    w1e = psb("w1e", [128, NCH * 1024], BF16)
    w2e = psb("w2e", [128, 2 * NCH * 1024], BF16)
    y_acc = psb("y_acc", [128, NCH * PW], F32)
    a_sb = psb("a_sb", [128, 2 * NCH * 512], BF16)
    t1u = psb("t1u", [128, 4 * 512], BF16)
    stn2 = psb("stn2", [1, 3 * 512], F32)
    stb2 = psb("stb2", [1, 1024], BF16)
    of_sb = psb("of", [128, 2 * 512], F32)
    otok = psb("otok", [128, 4 * 1024], F32)
    rtr = psb("rtr", [8, 2 * 512], F32)
    ln2scr = psb("ln2scr", [128, 2 * 512], BF16)
    ln2bc = psb("ln2bc", [128, 2 * 512], BF16)
    ln2mb = psb("ln2mb", [128, 2 * 512], BF16)
    ln2t = psb("ln2t", [128, 8 * 512], BF16)
    tpk = psb("tpk", [128, 32], F32)
    selp = psb("sel_sb", [E, E * 128], BF16)
    ps_gb = _PsumRole(pst("ps_gb", [128, 512], F32))
    ps_hh = [_PsumRole(pst("ps_h0", [128, 512], F32)),
             _PsumRole(pst("ps_h1", [128, 512], F32))]
    ps_yy = [_PsumRole(pst("ps_y0", [128, 512], F32)),
             _PsumRole(pst("ps_y1", [128, 512], F32))]
    ps_b = [_PsumRole(pst("ps_b0", [128, 512], F32)),
            _PsumRole(pst("ps_b1", [128, 512], F32)),
            _PsumRole(pst("ps_b2", [128, 512], F32))]
    sn2 = lambda i: stn2[:, i * 512:(i + 1) * 512]
    rt = lambda i: rtr[:, i * 512:(i + 1) * 512]
    tp3 = tpk.rearrange("p (g k) -> p g k", k=E)
    g3 = lambda a: a.rearrange("p (g o) -> p g o", o=1).broadcast_to([128, 4, E])
    t_m1, t_m2 = of_sb[:, 0:4], of_sb[:, 4:8]
    t_s, t_rs = of_sb[:, 8:12], of_sb[:, 12:16]
    t_mlt, t_pm = of_sb[:, 512:544], of_sb[:, 544:576]
    t_kp, t_kt = of_sb[:, 576:608], of_sb[:, 608:640]

    ldb()
    dma("gpsimd", selp[:], sel_h[:], s_ld, writes=[_h(selp)])
    v_sel = s_ld.n

    slice_ready = [0, 0]
    slice_guard = [None, None]
    h_done = {}
    r2_done = {}

    def load_slice(Hh_):
        rp = Hh_ % 2
        p.wait("sync", s_hw, s_hw.n)
        if slice_guard[rp] is not None:
            for w_ in slice_guard[rp]:
                p.wait("sync", *w_)
        dma("sync",
            r1n_ps[:, rp * NCH * PW:(rp + 1) * NCH * PW].rearrange("p (c t) -> p c t", t=PW),
            r1n_hbm.rearrange("p (c t) -> p c t", t=T)[:, :, Hh_ * PW:(Hh_ + 1) * PW],
            s_hw, writes=[_h(r1n_ps)])
        slice_ready[rp] = s_hw.n
        p.wait("sync", s_hw, s_hw.n)

    gates_ready = {}

    def seg_router(Hh_):
        rp = Hh_ % 2
        p.wait("tensor", s_hw, slice_ready[rp])
        for sl in range(PW // SLAB):
            ps_mm0 = ps_hh[0]
            ps_mm0.wait_writer(p)
            for kc in range(NCH):
                v = p.do("tensor",
                         lambda En, o=ps_mm0.t, l=rwt_sb[:, kc * E:(kc + 1) * E],
                         r=r1n_ps[:, rp * NCH * PW + kc * PW + sl * SLAB: rp * NCH * PW + kc * PW + (sl + 1) * SLAB],
                         a=(kc == 0), b=(kc == NCH - 1):
                         En.matmul(o[0:E, :], l, r, start=a, stop=b),
                         inc=(s_pe if kc == NCH - 1 else None))
            p.wait("scalar", s_pe, v)
            va = p.do("scalar", lambda En: En.activation(rt(0), ps_mm0.t[0:E, :], AF.Exp, scale=-1.0),
                      reads=[_h(ps_mm0.t)], writes=[_h(rtr)], inc=s_act)
            ps_mm0.consumed(s_act, va)
            p.do("scalar", lambda En: En.activation(rt(1), rt(0), AF.Ln, bias=1.0),
                 reads=[_h(rtr)], writes=[_h(rtr)], inc=s_act)
            v_p = p.do("scalar", lambda En: En.activation(rt(0), rt(1), AF.Exp, scale=-1.0),
                       reads=[_h(rtr)], writes=[_h(rtr)], inc=s_act)
            p.wait("tensor", s_act, v_p)
            ps_mm1 = ps_hh[1]
            ps_mm1.wait_writer(p)
            for g in range(4):
                v = p.do("tensor",
                         lambda En, o=ps_mm1.t, g_=g,
                         src=rtr[:, g * 128: (g + 1) * 128]:
                         En.transpose(o[:, g_ * 8:(g_ + 1) * 8], src, id32[0:E, 0:E]),
                         inc=(s_pe if g == 3 else None))
            p.wait("vector", s_pe, v)
            va = p.do("vector", lambda En: En.tensor_copy(tpk[:], ps_mm1.t[:, 0:32]),
                      reads=[_h(ps_mm1.t)], writes=[_h(tpk)], inc=s_dve)
            ps_mm1.consumed(s_dve, va)
            p.do("vector", lambda En: En.reduce_max(t_m1, tp3, axis=AX.X),
                 reads=[_h(tpk)], writes=[_h(of_sb)], inc=s_dve)
            p.do("vector", lambda En: En.tensor_tensor(t_mlt.rearrange("p (g k) -> p g k", k=E),
                                                       tp3, g3(t_m1), ALU.is_lt),
                 reads=[_h(tpk), _h(of_sb)], writes=[_h(of_sb)], inc=s_dve)
            p.do("vector", lambda En: En.tensor_mul(t_pm, tpk[:], t_mlt),
                 reads=[_h(tpk), _h(of_sb)], writes=[_h(of_sb)], inc=s_dve)
            p.do("vector", lambda En: En.reduce_max(t_m2, t_pm.rearrange("p (g k) -> p g k", k=E), axis=AX.X),
                 reads=[_h(of_sb)], writes=[_h(of_sb)], inc=s_dve)
            p.do("vector", lambda En: En.tensor_tensor(t_kp.rearrange("p (g k) -> p g k", k=E),
                                                       tp3, g3(t_m2), ALU.is_ge),
                 reads=[_h(tpk), _h(of_sb)], writes=[_h(of_sb)], inc=s_dve)
            p.do("vector", lambda En: En.tensor_mul(t_kt, tpk[:], t_kp),
                 reads=[_h(tpk), _h(of_sb)], writes=[_h(of_sb)], inc=s_dve)
            p.do("vector", lambda En: En.reduce_sum(t_s, t_kt.rearrange("p (g k) -> p g k", k=E), axis=AX.X),
                 reads=[_h(of_sb)], writes=[_h(of_sb)], inc=s_dve)
            p.do("vector", lambda En: En.tensor_scalar_add(t_s, t_s, 1e-20),
                 reads=[_h(of_sb)], writes=[_h(of_sb)], inc=s_dve)
            p.do("vector", lambda En: En.reciprocal(t_rs, t_s),
                 reads=[_h(of_sb)], writes=[_h(of_sb)], inc=s_dve)
            v_g = p.do("vector", lambda En: En.tensor_tensor(t_kt.rearrange("p (g k) -> p g k", k=E),
                                                             t_kt.rearrange("p (g k) -> p g k", k=E),
                                                             g3(t_rs), ALU.mult),
                       reads=[_h(of_sb)], writes=[_h(of_sb)], inc=s_dve)
            p.wait("tensor", s_dve, v_g)
            ps_b[1].wait_writer(p)
            for g in range(4):
                v = p.do("tensor",
                         lambda En, o=ps_b[1].t, g_=g, src=t_kt[:, g * 8:(g + 1) * 8]:
                         En.transpose(o[0:E, g_ * 128:(g_ + 1) * 128], src, id32[:]),
                         inc=(s_pe if g == 3 else None))
            p.wait("vector", s_pe, v)
            va = p.do("vector",
                      lambda En, o=gates[:, Hh_ * PW + sl * SLAB: Hh_ * PW + (sl + 1) * SLAB]:
                      En.tensor_copy(o, ps_b[1].t[0:E, 0:512]),
                      reads=[_h(ps_b[1].t)], writes=[_h(gates)], inc=s_dve)
            ps_b[1].consumed(s_dve, va)




    w_free = [None, None]
    w_free2 = [None, None]
    h_mm_done = {}
    a_free = [None, None]
    t1_free = [None, None]
    sqb_guard = [None, None]
    of_free = [None, None]
    otok_free = [None]
    n_store = 0
    ln2_freed = {}
    state = {"n_store": 0}

    def stream_w1(e):
        ldb()
        if w_free[0] is not None:
            p.wait("gpsimd", *w_free[0])
        dma("gpsimd", w1e.rearrange("p (c n) -> p c n", n=1024),
            w1.rearrange("(c p) n -> p c n", p=128)[:, :, e * 1024:(e + 1) * 1024],
            s_ld, writes=[_h(w1e)])
        return s_ld.n

    def stream_w2(e):
        ldb()
        sl = e % 2
        if w_free2[sl] is not None:
            p.wait("gpsimd", *w_free2[sl])
        dma("gpsimd",
            w2e[:, sl * NCH * 1024:(sl + 1) * NCH * 1024].rearrange("p (c n) -> p c n", n=1024),
            w2[e * 1024:(e + 1) * 1024, :].rearrange("(c p) n -> p c n", p=128),
            s_ld, writes=[_h(w2e)])
        return s_ld.n

    def emit_y(st_):
        apar, vb, v_w2e, tt, Hh_, e_ = st_
        w2b = (e_ % 2) * NCH * 1024
        p.wait("tensor", s_dve, vb)
        p.wait("tensor", s_ld, v_w2e)
        for ydc in range(NCH):
            role = ps_yy[ydc % 2]
            role.wait_writer(p)
            for wc in range(NCH):
                lhs = w2e[:, w2b + wc * 1024 + ydc * 128: w2b + wc * 1024 + ydc * 128 + 128]
                rhs = a_sb[:, apar * NCH * 512 + wc * 512: apar * NCH * 512 + (wc + 1) * 512]
                v = p.do("tensor",
                         lambda En, o=role.t, l=lhs, r=rhs,
                         a=(wc == 0), b=(wc == NCH - 1):
                         En.matmul(o[:], l, r, start=a, stop=b),
                         inc=(s_pe if wc == NCH - 1 else None))
            p.wait("vector", s_pe, v)
            if e_ == 0 and (tt, ydc) in ln2_freed:
                dv, av = ln2_freed[(tt, ydc)]
                p.wait("vector", s_dve, dv)
                p.wait("vector", s_act, av)
            ydst = y_acc[:, ydc * PW + tt * TT: ydc * PW + tt * TT + TT]
            if e_ == 0:
                va = p.do("vector", lambda En, o=ydst, i_=role.t: En.tensor_copy(o, i_[:]),
                          reads=[_h(role.t)], writes=[_h(y_acc)], inc=s_dve)
            else:
                va = p.do("vector", lambda En, o=ydst, i_=role.t: En.tensor_add(o, o, i_[:]),
                          reads=[_h(role.t), _h(y_acc)], writes=[_h(y_acc)], inc=s_dve)
            role.consumed(s_dve, va)
        a_free[apar] = (s_pe, v)

    ln2st = {}
    ln2mb_guard = [0]

    a_r2v = {}

    def seg_ln2_a_r2(Hh_, tt):
        rp_ = Hh_ % 2
        for dc in range(NCH):
            ydst = y_acc[:, dc * PW + tt * TT: dc * PW + tt * TT + TT]
            a_r2v[(tt, dc)] = p.do("vector",
                lambda En, o=ydst,
                b=r1n_ps[:, rp_ * NCH * PW + dc * PW + tt * TT: rp_ * NCH * PW + dc * PW + (tt + 1) * TT]:
                En.tensor_add(o, o, b),
                reads=[_h(y_acc), _h(r1n_ps)], writes=[_h(y_acc)], inc=s_dve)
        r2_done[Hh_] = s_dve.n

    def seg_ln2_a(Hh_, tt):
        ps_b[1].wait_writer(p)
        ps_b[2].wait_writer(p)
        v2s = 0
        vc = 0
        for dc in range(NCH):
            ydst = y_acc[:, dc * PW + tt * TT: dc * PW + tt * TT + TT]
            rb = ln2scr[:, (dc % 2) * 512:(dc % 2) * 512 + 512]
            p.wait("scalar", s_dve, a_r2v[(tt, dc)])
            if sqb_guard[dc % 2] is not None:
                p.wait("scalar", *sqb_guard[dc % 2])
            vc = p.do("scalar", lambda En, o=rb, i_=ydst: En.copy(o, i_),
                      reads=[_h(y_acc)], writes=[_h(ln2scr)], inc=s_act)
            p.wait("tensor", s_act, vc)
            p.do("tensor",
                 lambda En, o=ps_b[1].t, r=rb, a=(dc == 0), b=(dc == NCH - 1):
                 En.matmul(o[0:1, :], onesv[:], r, start=a, stop=b),
                 inc=s_pe)
            sq2 = ln2bc[:, (dc % 2) * 512:(dc % 2) * 512 + 512]
            vs = p.do("scalar", lambda En, o=sq2, i_=rb: En.activation(o, i_, AF.Square),
                      reads=[_h(ln2scr)], writes=[_h(ln2bc)], inc=s_act)
            p.wait("tensor", s_act, vs)
            v2s = p.do("tensor",
                       lambda En, o=ps_b[2].t, r=sq2, a=(dc == 0), b=(dc == NCH - 1):
                       En.matmul(o[0:1, :], onesv[:], r, start=a, stop=b),
                       inc=s_pe)
            sqb_guard[dc % 2] = (s_pe, v2s)
        ln2st["v2s"] = v2s
        ln2st["v_casts"] = vc

    def seg_ln2_b_chain(Hh_, tt):
        p.wait("vector", s_pe, ln2st["v2s"])
        p.wait("vector", s_act, ln2st.get("v_stb2", 0))
        p.do("vector", lambda En: En.tensor_scalar_mul(sn2(0), ps_b[1].t[0:1, :], 1.0 / D),
             reads=[_h(ps_b[1].t)], writes=[_h(stn2)], inc=s_dve)
        p.do("vector", lambda En: En.tensor_scalar_mul(sn2(1), ps_b[2].t[0:1, :], 1.0 / D),
             reads=[_h(ps_b[2].t)], writes=[_h(stn2)], inc=s_dve)
        p.do("vector", lambda En: En.tensor_mul(sn2(2), sn2(0), sn2(0)),
             reads=[_h(stn2)], writes=[_h(stn2)], inc=s_dve)
        v_var = p.do("vector", lambda En: En.tensor_sub(sn2(1), sn2(1), sn2(2)),
                     reads=[_h(stn2)], writes=[_h(stn2)], inc=s_dve)
        ps_b[1].consumed(s_dve, v_var)
        ps_b[2].consumed(s_dve, v_var)
        p.wait("scalar", s_dve, v_var)
        p.do("scalar", lambda En: En.activation(sn2(2), sn2(1), AF.Ln, bias=LN_EPS),
             reads=[_h(stn2)], writes=[_h(stn2)], inc=s_act)
        v_rs = p.do("scalar", lambda En: En.activation(sn2(1), sn2(2), AF.Exp, scale=-0.5),
                    reads=[_h(stn2)], writes=[_h(stn2)], inc=s_act)
        p.wait("scalar", s_pe, ln2st.get("v_bc_pe", 0))
        p.do("scalar", lambda En: En.copy(stb2[:, 0:512], sn2(0)),
             reads=[_h(stn2)], writes=[_h(stb2)], inc=s_act)
        v_stb2 = p.do("scalar", lambda En: En.copy(stb2[:, 512:1024], sn2(1)),
                      reads=[_h(stn2)], writes=[_h(stb2)], inc=s_act)
        ln2st["v_stb2"] = v_stb2

    def seg_ln2_b_bcast(Hh_, tt):
        p.wait("tensor", s_act, ln2st["v_stb2"])
        ps_b[1].wait_writer(p)
        p.do("tensor", lambda En: En.matmul(ps_b[1].t[:], onescol[0:1, :],
                                            stb2[0:1, 0:512], start=True, stop=True),
             inc=s_pe)
        ps_b[2].wait_writer(p)
        v = p.do("tensor", lambda En: En.matmul(ps_b[2].t[:], onescol[0:1, :],
                                                stb2[0:1, 512:1024], start=True, stop=True),
                 inc=s_pe)
        ln2st["v_bc_pe"] = v
        p.wait("scalar", s_pe, v)
        p.wait("scalar", s_dve, ln2mb_guard[0])
        p.do("scalar", lambda En: En.copy(ln2mb[:, 0:512], ps_b[1].t[:]),
             reads=[_h(ps_b[1].t)], writes=[_h(ln2mb)], inc=s_act)
        va = p.do("scalar", lambda En: En.copy(ln2mb[:, 512:1024], ps_b[2].t[:]),
                  reads=[_h(ps_b[2].t)], writes=[_h(ln2mb)], inc=s_act)
        ps_b[1].consumed(s_act, va)
        ps_b[2].consumed(s_act, va)
        ln2st["v_mb"] = va

    def seg_ln2_c_subs(Hh_, tt):
        p.wait("vector", s_act, ln2st["v_mb"])
        for dc in range(NCH):
            ysrc = y_acc[:, dc * PW + tt * TT: dc * PW + tt * TT + TT]
            t_ = ln2t[:, dc * 512:(dc + 1) * 512]
            v_sub = p.do("vector",
                         lambda En, o=t_, a=ysrc: En.tensor_sub(o, a, ln2mb[:, 0:512]),
                         reads=[_h(y_acc), _h(ln2mb)], writes=[_h(ln2t)], inc=s_dve)
            ln2_freed[(tt, dc)] = (v_sub, ln2st["v_casts"])

    def seg_ln2_c_muls(Hh_, tt):
        tok0 = Hh_ * PW + tt * TT
        if otok_free[0] is not None:
            p.wait("vector", *otok_free[0])
        for dc in range(NCH):
            p.do("vector",
                 lambda En, o=otok[:, dc * 512:(dc + 1) * 512],
                 a=ln2t[:, dc * 512:(dc + 1) * 512]:
                 En.tensor_mul(o, a, ln2mb[:, 512:1024]),
                 reads=[_h(ln2t), _h(ln2mb)], writes=[_h(otok)], inc=s_dve)
        v_otok = s_dve.n
        ln2mb_guard[0] = v_otok
        p.wait("sync", s_dve, v_otok)
        p.do("sync",
             lambda En,
             o=out_h.rearrange("p (c t) -> p c t", t=T)[:, :, tok0: tok0 + TT],
             i_=otok[:, 0:NCH * 512].rearrange("p (c t) -> p c t", t=512):
             En.dma_start(out=o, in_=i_),
             inc=s_hw, inc_by=16)
        state["n_store"] += 1
        otok_free[0] = (s_hw, s_hw.n)


    p.wait("tensor", s_ld, v_sel)
    ln2_pending = []
    load_slice(0)
    seg_router(0)
    gates_ready[0] = s_dve.n
    for Hh in range(NPASS):
        rp = Hh % 2
        v_w1 = stream_w1(0)
        if Hh == 0:
            v_w2 = stream_w2(0)
            pending = None
        v_h_last = 0
        for e in range(E):
            for tt in range(NTT):
                tok0 = Hh * PW + tt * TT
                apar = (e * NTT + tt) % 2
                gbank = ps_gb if apar == 0 else ps_b[0]
                gbank.wait_writer(p)
                if e == 0 and tt == 0:
                    p.wait("tensor", s_dve, gates_ready[Hh])
                v_gb = p.do("tensor",
                            lambda En, o=gbank.t, l=selp[:, e * 128:(e + 1) * 128],
                            r=gates[:, tok0: tok0 + TT]:
                            En.matmul(o[:], l, r, start=True, stop=True), inc=s_pe)
                p.wait("tensor", s_ld, v_w1)
                if a_free[apar] is not None:
                    p.wait("vector", *a_free[apar])
                for hdc in range(NCH):
                    role = ps_hh[hdc % 2]
                    role.wait_writer(p)
                    for kc in range(NCH):
                        lhs = w1e[:, kc * 1024 + hdc * 128: kc * 1024 + hdc * 128 + 128]
                        rhs = r1n_ps[:, rp * NCH * PW + kc * PW + tt * TT: rp * NCH * PW + kc * PW + (tt + 1) * TT]
                        v = p.do("tensor",
                                 lambda En, o=role.t, l=lhs, r=rhs,
                                 a=(kc == 0), b=(kc == NCH - 1):
                                 En.matmul(o[:], l, r, start=a, stop=b),
                                 inc=(s_pe if kc == NCH - 1 else None))
                    p.wait("scalar", s_pe, v)
                    t1 = t1u[:, (hdc % 2) * 512:(hdc % 2) * 512 + 512]
                    u = t1u[:, 1024 + (hdc % 2) * 512: 1024 + (hdc % 2) * 512 + 512]
                    if t1_free[hdc % 2] is not None:
                        p.wait("scalar", *t1_free[hdc % 2])
                    va = p.do("scalar", lambda En, o=t1, i_=role.t: En.activation(o, i_[:], AF.Relu),
                              reads=[_h(role.t)], writes=[_h(t1u)], inc=s_act)
                    role.consumed(s_act, va)
                    p.wait("vector", s_act, va)
                    p.wait("vector", s_pe, v_gb)
                    p.do("vector", lambda En, o=u, a=t1, gb_=gbank.t: En.tensor_mul(o, a, gb_[:]),
                         reads=[_h(t1u), _h(gbank.t)], writes=[_h(t1u)], inc=s_dve)
                    vb = p.do("vector",
                              lambda En, o=a_sb[:, apar * NCH * 512 + hdc * 512: apar * NCH * 512 + (hdc + 1) * 512],
                              a=u, b=t1: En.tensor_mul(o, a, b),
                              reads=[_h(t1u)], writes=[_h(a_sb)], inc=s_dve)
                    t1_free[hdc % 2] = (s_dve, vb)
                gbank.consumed(s_dve, vb)
                if tt == NTT - 1:
                    v_h_last = s_pe.n
                prev = pending
                boundary = (prev is not None and prev[5] == E - 1
                            and prev[3] == NTT - 1)
                pre, post = ln2_pending.pop(0) if ln2_pending else ((), ())
                if boundary:
                    emit_y(prev)
                    w_free2[prev[5] % 2] = (s_pe, s_pe.n)
                    v_w2 = stream_w2(0)
                for f in pre:
                    f()
                if prev is not None and not boundary:
                    emit_y(prev)
                    w_free2[prev[5] % 2] = (s_pe, s_pe.n)
                    if prev[5] == e - 1 and prev[3] == NTT - 1 and e > 0:
                        v_w2 = stream_w2(e)
                for f in post:
                    f()
                if Hh == NPASS - 1 and e == E - 1 and tt == NTT - 1 and NTT >= 2:
                    seg_ln2_a_r2(Hh, 0)
                    seg_ln2_a(Hh, 0)
                    seg_ln2_b_chain(Hh, 0)
                pending = (apar, vb, v_w2, tt, Hh, e)
            if e == 1 and Hh + 1 < NPASS:
                if Hh >= 1:
                    slice_guard[(Hh + 1) % 2] = [(s_pe, h_done[Hh - 1]),
                                                 (s_dve, r2_done[Hh - 1])]
                load_slice(Hh + 1)
            if e == 3 and Hh + 1 < NPASS:
                seg_router(Hh + 1)
                gates_ready[Hh + 1] = s_dve.n
            if e + 1 < E:
                w_free[0] = (s_pe, v_h_last)
                v_w1 = stream_w1(e + 1)
        h_done[Hh] = v_h_last
        w_free[0] = (s_pe, v_h_last)
        mk = lambda fn, tt: (lambda Hh_=Hh, tt_=tt: fn(Hh_, tt_))
        if Hh == NPASS - 1 and NTT >= 2:
            ln2_pending = [([mk(seg_ln2_a_r2, NTT - 1)], [])]
        else:
            ln2_pending = [([mk(seg_ln2_a_r2, t_) for t_ in range(NTT)]
                            + [mk(seg_ln2_a, 0), mk(seg_ln2_b_chain, 0)], [])]
        for t_ in range(1, NTT):
            ln2_pending.append(([mk(seg_ln2_b_bcast, t_ - 1), mk(seg_ln2_c_subs, t_ - 1)],
                                [mk(seg_ln2_c_muls, t_ - 1),
                                 mk(seg_ln2_a, t_), mk(seg_ln2_b_chain, t_)]))
        ln2_pending.append(([mk(seg_ln2_b_bcast, NTT - 1), mk(seg_ln2_c_subs, NTT - 1)],
                            [mk(seg_ln2_c_muls, NTT - 1)]))
    emit_y(pending)
    pending = None
    for pre, post in ln2_pending:
        for f in pre:
            f()
        for f in post:
            f()

    p.wait("sync", s_hw, s_hw.n)
    p.flush()
    ph.close()
    st.close()
    return nc


def _host_consts():
    spk = np.zeros((128, NCH * NH), dtype=np.float32)
    for c in range(NCH):
        for pp in range(128):
            spk[pp, c * NH + 2 * c + (1 if pp >= 64 else 0)] = 1.0
    bpk = np.zeros((NH, NCH * 128), dtype=np.float32)
    for c in range(NCH):
        for i in range(128):
            bpk[2 * c + (1 if i >= 64 else 0), c * 128 + i] = 1.0
    sel = np.zeros((E, E * 128), dtype=np.float32)
    for e in range(E):
        sel[e, e * 128:(e + 1) * 128] = 1.0
    ones = np.ones((1, 128), dtype=np.float32)
    id128 = np.eye(128, dtype=np.float32)
    return dict(spk=spk, spkn=-spk, bpk=bpk, selpk=sel, onespk=ones,
                onesv128=np.ones((128, 1), dtype=np.float32), id128=id128)


def make_in_maps(x, Wp, bp, Wq, bq, Wk, bk, Wv, bv, Wo, bo, router_w, w1, w2,
                 n_cores, T):
    B, S, Dm = x.shape
    per_seq = n_cores // B
    consts = _host_consts()
    c32 = lambda a: np.ascontiguousarray(a, dtype=np.float32)
    fm = lambda b: c32(np.reshape(np.asarray(b), (NCH, 128)).T)
    rwt_fm = c32(np.transpose(router_w).reshape(NCH, 128, E).transpose(1, 0, 2).reshape(128, NCH * E))
    shared = dict(
        wp=c32(Wp), wqt=c32(np.transpose(Wq)), wkt=c32(np.transpose(Wk)),
        wvt=c32(np.transpose(Wv)), wot=c32(np.transpose(Wo)),
        bp=fm(bp), bq=fm(bq), bk=fm(bk), bv=fm(bv), bo=fm(bo),
        rwt=rwt_fm, w1=c32(w1), w2=c32(w2), **consts)
    in_maps = []
    for c in range(n_cores):
        b = c // per_seq
        h = c % per_seq
        a0 = h * T
        lo = (a0 - 1) % S
        hi = (a0 + T) % S
        xsh = np.concatenate([x[b, lo:lo + 1, :], x[b, a0:a0 + T, :],
                              x[b, hi:hi + 1, :]], axis=0)
        m = dict(shared)
        m["xs"] = c32(xsh)
        in_maps.append(m)
    return in_maps


def kernel(L0_attn_x, L0_attn_Wp, L0_attn_bp, L0_attn_Wq, L0_attn_bq,
           L0_attn_Wk, L0_attn_bk, L0_attn_Wv, L0_attn_bv, L0_attn_Wo,
           L0_attn_bo, L0_ffn_router_w, L0_ffn_w1, L0_ffn_w2):
    x = np.asarray(L0_attn_x)
    B, S, Dm = x.shape
    n_cores = 8
    T = B * S // n_cores
    in_maps = make_in_maps(x, L0_attn_Wp, L0_attn_bp, L0_attn_Wq, L0_attn_bq,
                           L0_attn_Wk, L0_attn_bk, L0_attn_Wv, L0_attn_bv,
                           L0_attn_Wo, L0_attn_bo, L0_ffn_router_w, L0_ffn_w1,
                           L0_ffn_w2, n_cores, T)
    nc = build_program(T)
    res = run_bass_kernel_spmd(nc, in_maps, core_ids=list(range(n_cores)))
    shard0 = np.asarray(res.results[0]["out"])
    out = np.empty((B, S, Dm), dtype=shard0.dtype)
    per_seq = n_cores // B
    for c in range(n_cores):
        b = c // per_seq
        h = c % per_seq
        sh = np.asarray(res.results[c]["out"]).reshape(128, NCH, T)
        out[b, h * T:(h + 1) * T, :] = sh.transpose(2, 1, 0).reshape(T, Dm)
    return out


# version 40
# speedup vs baseline: 1.0077x; 1.0077x over previous
import numpy as np
from contextlib import ExitStack

import concourse.bass as bass
import concourse.mybir as mybir
from concourse.bass_utils import run_bass_kernel_spmd

F32 = mybir.dt.float32
BF16 = mybir.dt.bfloat16
AF = mybir.ActivationFunctionType
ALU = mybir.AluOpType
AX = mybir.AxisListType

D = 1024
NCH = 8
NH = 16
E = 8
SLAB = 512
TT = 512
LN_EPS = 1e-5

ENGS = ("tensor", "vector", "scalar", "gpsimd", "sync")


class _Sem:
    def __init__(self, handle):
        self.h = handle
        self.n = 0


class _Prog:

    def __init__(self, block):
        self.block = block
        self.ops = {e: [] for e in ENGS}
        self.dirty = {e: set() for e in ENGS}
        self.read = {e: set() for e in ENGS}

    def wait(self, eng, sem, val):
        if val <= 0:
            return
        self.ops[eng].append(lambda E, s=sem, v=val: E.wait_ge(s.h, v))

    def _maybe_drain(self, eng, reads, writes):
        if eng in ("vector", "scalar", "gpsimd"):
            r = {id(h) for h in reads}
            w = {id(h) for h in writes}
            if ((r | w) & self.dirty[eng]) or (w & self.read[eng]):
                self.ops[eng].append(lambda E: E.drain())
                self.dirty[eng].clear()
                self.read[eng].clear()

    def do(self, eng, fn, reads=(), writes=(), inc=None, inc_by=1):
        self._maybe_drain(eng, tuple(reads), tuple(writes))
        if inc is None:
            self.ops[eng].append(lambda E, f=fn: f(E))
        else:
            inc.n += inc_by
            self.ops[eng].append(lambda E, f=fn, s=inc, k=inc_by: f(E).then_inc(s.h, k))
        for h in writes:
            self.dirty[eng].add(id(h))
        for h in reads:
            self.read[eng].add(id(h))
        return inc.n if inc is not None else None

    def flush(self):
        nc_ = self.block.bass
        for eng in ENGS:
            ops = self.ops[eng]
            if not ops:
                continue
            self.ops[eng] = []

            def replay(engine, ops_=ops):
                with nc_.allow_low_precision("bf16 elementwise adds are within tolerance"):
                    for f in ops_:
                        f(engine)

            getattr(self.block, eng)(replay)


class _PsumRole:
    def __init__(self, t):
        self.t = t
        self.last = None

    def wait_writer(self, prog):
        if self.last is not None:
            prog.wait("tensor", *self.last)

    def consumed(self, sem, val):
        self.last = (sem, val)


def _h(ap):
    return ap.tensor if hasattr(ap, "tensor") else ap


def build_program(T):
    assert T % SLAB == 0
    NSLAB = T // SLAB
    NPASS = max(2, T // 1024)
    PW = T // NPASS
    NTT = PW // TT
    assert PW % TT == 0 and NTT >= 1
    XFW = SLAB + 2

    nc = bass.Bass("TRN2", target_bir_lowering=False)

    xs = nc.dram_tensor("xs", [T + 2, D], F32, kind="ExternalInput").ap()
    wp = nc.dram_tensor("wp", [D, D], F32, kind="ExternalInput").ap()
    wqt = nc.dram_tensor("wqt", [D, D], F32, kind="ExternalInput").ap()
    wkt = nc.dram_tensor("wkt", [D, D], F32, kind="ExternalInput").ap()
    wvt = nc.dram_tensor("wvt", [D, D], F32, kind="ExternalInput").ap()
    wot = nc.dram_tensor("wot", [D, D], F32, kind="ExternalInput").ap()
    bp_h = nc.dram_tensor("bp", [128, NCH], F32, kind="ExternalInput").ap()
    bq_h = nc.dram_tensor("bq", [128, NCH], F32, kind="ExternalInput").ap()
    bk_h = nc.dram_tensor("bk", [128, NCH], F32, kind="ExternalInput").ap()
    bv_h = nc.dram_tensor("bv", [128, NCH], F32, kind="ExternalInput").ap()
    bo_h = nc.dram_tensor("bo", [128, NCH], F32, kind="ExternalInput").ap()
    rwt = nc.dram_tensor("rwt", [128, NCH * E], F32, kind="ExternalInput").ap()
    w1 = nc.dram_tensor("w1", [D, E * 1024], F32, kind="ExternalInput").ap()
    w2 = nc.dram_tensor("w2", [E * 1024, D], F32, kind="ExternalInput").ap()
    id_h = nc.dram_tensor("id128", [128, 128], F32, kind="ExternalInput").ap()
    spk_h = nc.dram_tensor("spk", [128, NCH * NH], F32, kind="ExternalInput").ap()
    spkn_h = nc.dram_tensor("spkn", [128, NCH * NH], F32, kind="ExternalInput").ap()
    bpk_h = nc.dram_tensor("bpk", [NH, NCH * 128], F32, kind="ExternalInput").ap()
    sel_h = nc.dram_tensor("selpk", [E, E * 128], F32, kind="ExternalInput").ap()
    ones_h = nc.dram_tensor("onespk", [1, 128], F32, kind="ExternalInput").ap()
    onesv_h = nc.dram_tensor("onesv128", [128, 1], F32, kind="ExternalInput").ap()
    out_h = nc.dram_tensor("out", [128, NCH * T], F32, kind="ExternalOutput").ap()
    r1n_hbm = nc.dram_tensor("r1n_hbm", [128, NCH * T], BF16, kind="Internal").ap()

    st = ExitStack()
    sbt = lambda name, shape, dt: st.enter_context(nc.sbuf_tensor(name, shape, dt)).ap()
    semh = lambda name: _Sem(st.enter_context(nc.semaphore(name)))

    s_ld = semh("s_ld")
    s_pe = semh("s_pe")
    s_dve = semh("s_dve")
    s_act = semh("s_act")
    s_hw = semh("s_hw")

    block = st.enter_context(nc.Block())
    p = _Prog(block)

    rwt_sb = sbt("rwt_sb", [128, NCH * E], BF16)
    bo_sb = sbt("bo_sb", [128, NCH], F32)
    idb = sbt("idb", [128, 128], BF16)
    id32 = sbt("id32", [128, 128], F32)
    onescol = sbt("onescol", [1, 128], BF16)
    onesv = sbt("onesv", [128, 1], BF16)
    gates = sbt("gates", [E, T], BF16)
    cst_eps = sbt("cst_eps", [128, 1], F32)
    nc.const_aps.register(F32, LN_EPS, cst_eps)
    cst_one = sbt("cst_one", [128, 1], F32)
    nc.const_aps.register(F32, 1.0, cst_one)

    def dma(eng, out_ap, in_ap, sem_, writes=()):
        return p.do(eng, lambda En, o=out_ap, i_=in_ap: En.dma_start(out=o, in_=i_),
                    writes=writes, inc=sem_, inc_by=16)

    def ldb():
        p.wait("gpsimd", s_ld, s_ld.n)

    p.do("vector", lambda En: En.memset(cst_eps, LN_EPS), writes=[_h(cst_eps)], inc=s_dve)
    p.do("vector", lambda En: En.memset(cst_one, 1.0), writes=[_h(cst_one)], inc=s_dve)
    dma("gpsimd", idb[:], id_h[:], s_ld, writes=[_h(idb)])
    dma("gpsimd", id32[:], id_h[:], s_ld, writes=[_h(id32)])
    dma("gpsimd", onescol[:], ones_h[:], s_ld, writes=[_h(onescol)])
    dma("gpsimd", onesv[:], onesv_h[:], s_ld, writes=[_h(onesv)])
    dma("gpsimd", rwt_sb[:], rwt[:], s_ld, writes=[_h(rwt_sb)])
    dma("gpsimd", bo_sb[:], bo_h[:], s_ld, writes=[_h(bo_sb)])

    pa = ExitStack()
    asb = lambda name, shape, dt: pa.enter_context(nc.sbuf_tensor(name, shape, dt)).ap()
    cqt = asb("cqt", [128, NCH * 1024], BF16)
    ckt = asb("ckt", [128, NCH * 1024], BF16)
    cvt = asb("cvt", [128, NCH * 1024], BF16)
    wot_sb = asb("wot_sb", [128, NCH * 1024], BF16)
    cbq = asb("cbq", [128, NCH], F32)
    cbk = asb("cbk", [128, NCH], F32)
    cbv = asb("cbv", [128, NCH], F32)
    spk = asb("spk_sb", [128, NCH * NH], BF16)
    spkn = asb("spkn_sb", [128, NCH * NH], BF16)
    bpk = asb("bpk_sb", [NH, NCH * 128], BF16)

    dma("gpsimd", spk[:], spk_h[:], s_ld, writes=[_h(spk)])
    dma("gpsimd", spkn[:], spkn_h[:], s_ld, writes=[_h(spkn)])
    dma("gpsimd", bpk[:], bpk_h[:], s_ld, writes=[_h(bpk)])

    ph = ExitStack()
    psb = lambda name, shape, dt: ph.enter_context(nc.sbuf_tensor(name, shape, dt)).ap()
    pst = lambda name, shape, dt: ph.enter_context(nc.psum_tensor(name, shape, dt)).ap()
    wp_sb = psb("wp_sb", [128, NCH * 1024], BF16)
    wxt = psb("wxt_sb", [128, 2 * NCH * 1024], BF16)
    biasf = psb("bias_f", [128, 3 * NCH], F32)
    bpb = psb("bpb", [128, NCH], BF16)
    psf = [_PsumRole(pst("ps_f0", [128, 512], F32)),
           _PsumRole(pst("ps_f1", [128, 512], F32))]
    psfb = _PsumRole(pst("ps_fb", [128, 8], F32))

    dma("gpsimd", wp_sb.rearrange("p (c n) -> p c n", n=1024),
        wp.rearrange("(c p) n -> p c n", p=128), s_ld, writes=[_h(wp_sb)])
    dma("gpsimd", bpb, bp_h[:], s_ld, writes=[_h(bpb)])
    for i, bh in enumerate([bq_h, bk_h, bv_h]):
        dma("gpsimd", biasf[:, i * NCH:(i + 1) * NCH], bh[:], s_ld, writes=[_h(biasf)])
    base_ld = s_ld.n

    proj_done = {}
    for xi, (wxt_h, c_sb, cb_sb, scale) in enumerate(
            [(wqt, cqt, cbq, 0.125), (wkt, ckt, cbk, 1.0), (wvt, cvt, cbv, 1.0)]):
        par0 = (xi % 2) * NCH * 1024
        ldb()
        if xi >= 2:
            p.wait("gpsimd", s_pe, proj_done[xi - 2])
        v_wx = dma("gpsimd",
                   wxt[:, par0: par0 + NCH * 1024].rearrange("p (c n) -> p c n", n=1024),
                   wxt_h.rearrange("(c p) n -> p c n", p=128), s_ld, writes=[_h(wxt)])
        for dc in range(NCH):
            for jg in range(2):
                role = psf[(dc * 2 + jg) % 2]
                role.wait_writer(p)
                p.wait("tensor", s_ld, v_wx)
                for kc in range(NCH):
                    lhs = wp_sb[:, kc * 1024 + dc * 128: kc * 1024 + dc * 128 + 128]
                    rhs = wxt[:, par0 + kc * 1024 + jg * 512: par0 + kc * 1024 + jg * 512 + 512]
                    v = p.do("tensor",
                             lambda En, o=role.t, l=lhs, r=rhs, a=(kc == 0), b=(kc == NCH - 1):
                             En.matmul(o[:], l, r, start=a, stop=b),
                             inc=(s_pe if kc == NCH - 1 else None))
                p.wait("scalar", s_pe, v)
                dst = c_sb[:, dc * 1024 + jg * 512: dc * 1024 + jg * 512 + 512]
                va = p.do("scalar",
                          lambda En, o=dst, i_=role.t, sc=scale:
                          En.activation(o, i_[:], AF.Copy, scale=sc),
                          reads=[_h(role.t)], writes=[_h(c_sb)], inc=s_act)
                role.consumed(s_act, va)
        for jc in range(NCH):
            psfb.wait_writer(p)
            p.wait("tensor", s_ld, base_ld)
            for kc in range(NCH):
                lhs = wxt[:, par0 + kc * 1024 + jc * 128: par0 + kc * 1024 + jc * 128 + 128]
                v = p.do("tensor",
                         lambda En, o=psfb.t, l=lhs, r=bpb[:, kc:kc + 1],
                         a=(kc == 0), b=(kc == NCH - 1):
                         En.matmul(o[:, 0:1], l, r, start=a, stop=b),
                         inc=(s_pe if kc == NCH - 1 else None))
            p.wait("vector", s_pe, v)
            bcol = biasf[:, xi * NCH + jc: xi * NCH + jc + 1]
            va = p.do("vector",
                      lambda En, o=cb_sb[:, jc:jc + 1], i_=psfb.t, bb=bcol, sc=scale:
                      En.tensor_scalar(o, i_[:, 0:1], bb, sc, ALU.add, ALU.mult),
                      reads=[_h(psfb.t)], writes=[_h(cb_sb)], inc=s_dve)
            psfb.consumed(s_dve, va)
        proj_done[xi] = s_pe.n

    ldb()
    dma("gpsimd", wot_sb.rearrange("p (c n) -> p c n", n=1024),
        wot.rearrange("(c p) n -> p c n", p=128), s_ld, writes=[_h(wot_sb)])
    v_wot_ld = s_ld.n
    p.wait("gpsimd", s_pe, s_pe.n)
    p.wait("gpsimd", s_dve, s_dve.n)
    p.wait("gpsimd", s_act, s_act.n)
    p.flush()
    ph.close()

    p.wait("gpsimd", s_pe, s_pe.n)
    p.wait("gpsimd", s_dve, s_dve.n)
    p.wait("gpsimd", s_act, s_act.n)
    p.flush()
    ph.close()
    ph = ExitStack()
    psb = lambda name, shape, dt: ph.enter_context(nc.sbuf_tensor(name, shape, dt)).ap()
    pst = lambda name, shape, dt: ph.enter_context(nc.psum_tensor(name, shape, dt)).ap()
    xbt = psb("xbt", [128, 2 * 1024], BF16)
    xh2 = psb("xh2", [34, 1024], BF16)
    x_fm = psb("x_fm", [128, 2 * NCH * XFW], BF16)
    q_sb = psb("q_sb", [128, 2 * NCH * XFW], BF16)
    k_sb = psb("k_sb", [128, 2 * NCH * XFW], BF16)
    v_sb = psb("v_sb", [128, 2 * NCH * XFW], BF16)
    pscr = psb("pscr", [128, 2 * 512], BF16)
    pscr2 = psb("pscr2", [128, 2 * 512], BF16)
    smt = psb("smt", [16, 4 * 512], BF16)
    wpl = psb("wpl", [16, 3 * 512], BF16)
    sqscr = psb("sqscr", [128, 2 * 512], BF16)
    wbe = psb("wbe", [128, 6 * 512], BF16)
    stn = psb("stn", [1, 3 * 512], F32)
    stb = psb("stb", [1, 1024], BF16)
    r1n_stg = psb("r1n_stg", [128, 2 * NCH * SLAB], BF16)
    ps_tb = [_PsumRole(pst("ps_tb0", [128, 1024], BF16)),
             _PsumRole(pst("ps_tb1", [128, 1024], BF16))]
    ps_mm = [_PsumRole(pst("ps_mm0", [128, 512], F32)),
             _PsumRole(pst("ps_mm1", [128, 512], F32))]
    ps_s = [_PsumRole(pst("ps_s0", [128, 512], F32)),
            _PsumRole(pst("ps_s1", [128, 512], F32)),
            _PsumRole(pst("ps_s2", [128, 512], F32))]
    ps_st = _PsumRole(pst("ps_st", [128, 512], F32))

    PB = NCH * XFW
    xfc = lambda s, dc: (s % 2) * PB + dc * XFW
    r1c = lambda s, dc: q_sb[:, (s % 2) * PB + dc * XFW: (s % 2) * PB + dc * XFW + 512]
    mixc = lambda s, dc: k_sb[:, (s % 2) * PB + dc * XFW: (s % 2) * PB + dc * XFW + 512]
    sm = lambda i: smt[:, i * 512:(i + 1) * 512]
    wp_ = lambda i: wpl[:, i * 512:(i + 1) * 512]
    sn = lambda i: stn[:, i * 512:(i + 1) * 512]

    xtile_free = [None, None]
    xh_free = [None, None]
    g_qkvmm = [0, 0]
    g_r1 = [0, 0]
    g_wo = [0, 0]
    g_wsum = [0, 0]
    g_r1n = [0, 0]
    g_exp = 0
    g_mb = 0
    g_lnact = 0
    rstg_free = [None, None]
    pslots = [pscr[:, 0:512], pscr[:, 512:1024], pscr2[:, 0:512],
              pscr2[:, 512:1024], sqscr[:, 0:512], sqscr[:, 512:1024]]
    pslot_t = [_h(pscr), _h(pscr), _h(pscr2), _h(pscr2), _h(sqscr), _h(sqscr)]
    pslot_guard = [None] * 6
    wbe_guard = [None] * 6
    SS = {}

    def seg_xt(s):
        par = s % 2
        r0 = s * SLAB
        p.wait("scalar", s_dve, g_r1[par])
        p.wait("scalar", s_pe, g_qkvmm[par])
        for tt in range(4):
            tb = xbt[:, (tt % 2) * 1024:(tt % 2) * 1024 + 1024]
            ldb()
            if xtile_free[tt % 2] is not None:
                p.wait("gpsimd", *xtile_free[tt % 2])
            v_ld = dma("gpsimd", tb, xs[r0 + tt * 128: r0 + (tt + 1) * 128, :], s_ld,
                       writes=[_h(xbt)])
            role = ps_tb[tt % 2]
            role.wait_writer(p)
            p.wait("tensor", s_ld, v_ld)
            for dc in range(NCH):
                v = p.do("tensor",
                         lambda En, o=role.t, d_=dc, src=tb[:, dc * 128:(dc + 1) * 128]:
                         En.transpose(o[:, d_ * 128:(d_ + 1) * 128], src, idb[:]),
                         inc=(s_pe if dc == NCH - 1 else None))
            xtile_free[tt % 2] = (s_pe, v)
            p.wait("scalar", s_pe, v)
            dst = x_fm[:, par * PB: (par + 1) * PB].rearrange(
                "p (c n) -> p c n", n=XFW)[:, :, tt * 128:(tt + 1) * 128]
            va = p.do("scalar",
                      lambda En, o=dst, i_=role.t:
                      En.copy(o, i_[:].rearrange("p (c n) -> p c n", n=128)),
                      reads=[_h(role.t)], writes=[_h(x_fm)], inc=s_act)
            role.consumed(s_act, va)
        ldb()
        if xh_free[par] is not None:
            p.wait("gpsimd", *xh_free[par])
        v_ld = dma("gpsimd", xh2[par * 32: par * 32 + 2, :], xs[r0 + 512: r0 + 514, :],
                   s_ld, writes=[_h(xh2)])
        role = ps_tb[0]
        role.wait_writer(p)
        p.wait("tensor", s_ld, v_ld)
        for dc in range(NCH):
            v = p.do("tensor",
                     lambda En, o=role.t, d_=dc,
                     src=xh2[par * 32: par * 32 + 2, dc * 128:(dc + 1) * 128],
                     ii=idb[par * 32: par * 32 + 2, par * 32: par * 32 + 2]:
                     En.transpose(o[:, d_ * 2: d_ * 2 + 2], src, ii),
                     inc=(s_pe if dc == NCH - 1 else None))
        xh_free[par] = (s_pe, v)
        p.wait("scalar", s_pe, v)
        dst = x_fm[:, par * PB: (par + 1) * PB].rearrange(
            "p (c n) -> p c n", n=XFW)[:, :, 512:514]
        va = p.do("scalar",
                  lambda En, o=dst, i_=role.t:
                  En.copy(o, i_[:, 0:2 * NCH].rearrange("p (c n) -> p c n", n=2)),
                  reads=[_h(role.t)], writes=[_h(x_fm)], inc=s_act)
        role.consumed(s_act, va)
        SS[("xfm", s)] = s_act.n

    def seg_qkv(s):
        par = s % 2
        p.wait("tensor", s_act, SS[("xfm", s)])
        p.wait("tensor", s_ld, v_wot_ld)
        p.wait("scalar", s_pe, g_wo[par])
        p.wait("scalar", s_dve, g_wsum[par])
        v_evac = {}
        gi = 0
        for ti, c_sb, cb_sb, dst_sb in ((1, ckt, cbk, k_sb), (2, cvt, cbv, v_sb),
                                        (0, cqt, cbq, q_sb)):
            if ti == 0:
                p.wait("scalar", s_dve, g_r1n[par])
            for dc in range(NCH):
                for gg in range(2):
                    g0, gw = (0, 257) if gg == 0 else (257, 257)
                    role = ps_mm[gi % 2]
                    gi += 1
                    role.wait_writer(p)
                    for kc in range(NCH):
                        lhs = c_sb[:, kc * 1024 + dc * 128: kc * 1024 + dc * 128 + 128]
                        rhs = x_fm[:, par * PB + kc * XFW + g0: par * PB + kc * XFW + g0 + gw]
                        v = p.do("tensor",
                                 lambda En, o=role.t, l=lhs, r=rhs, w_=gw,
                                 a=(kc == 0), b=(kc == NCH - 1):
                                 En.matmul(o[:, 0:w_], l, r, start=a, stop=b),
                                 inc=(s_pe if kc == NCH - 1 else None))
                    p.wait("scalar", s_pe, v)
                    dst = dst_sb[:, par * PB + dc * XFW + g0: par * PB + dc * XFW + g0 + gw]
                    va = p.do("scalar",
                              lambda En, o=dst, i_=role.t, w_=gw, bb=cb_sb[:, dc:dc + 1]:
                              En.activation(o, i_[:, 0:w_], AF.Identity, bias=bb),
                              reads=[_h(role.t)], writes=[_h(dst_sb)], inc=s_act)
                    role.consumed(s_act, va)
                    v_evac[(ti, dc)] = va
        g_qkvmm[par] = v
        SS[("evac", s)] = v_evac

    def seg_scores(s):
        par = s % 2
        v_evac = SS[("evac", s)]
        ps_s[0].wait_writer(p)
        ps_s[2].wait_writer(p)
        si = 0
        vmm = 0
        for dc in range(NCH):
            prods = {}
            for o in (0, 1, 2):
                slot = si % 6
                si += 1
                scr = pslots[slot]
                if pslot_guard[slot] is not None:
                    p.wait("vector", *pslot_guard[slot])
                p.wait("vector", s_act, max(v_evac[(0, dc)], v_evac[(1, dc)]))
                vv = p.do("vector",
                          lambda En, o_=scr,
                          a=q_sb[:, par * PB + dc * XFW + 1: par * PB + dc * XFW + 513],
                          b=k_sb[:, par * PB + dc * XFW + o: par * PB + dc * XFW + o + 512]:
                          En.tensor_mul(o_, a, b),
                          reads=[_h(q_sb), _h(k_sb)], writes=[pslot_t[slot]], inc=s_dve)
                prods[o] = (slot, scr, vv)
            for bank, terms in ((ps_s[0], ((0, spk), (1, spkn))),
                                (ps_s[2], ((2, spk), (1, spkn)))):
                for j, (o, stat) in enumerate(terms):
                    slot, scr, vv = prods[o]
                    p.wait("tensor", s_dve, vv)
                    vmm = p.do("tensor",
                               lambda En, ot=bank.t, l=stat[:, dc * NH:(dc + 1) * NH], r=scr,
                               a=(dc == 0 and j == 0), b=(dc == NCH - 1 and j == 1):
                               En.matmul(ot[0:NH, :], l, r, start=a, stop=b),
                               inc=s_pe)
                    pslot_guard[slot] = (s_pe, vmm)
        SS[("scores", s)] = vmm

    def seg_softmax_a(s):
        p.wait("scalar", s_pe, SS[("scores", s)])
        p.wait("scalar", s_act, g_exp)
        p.wait("scalar", s_pe, g_mb)
        p.do("scalar", lambda En: En.activation(sm(0), ps_s[0].t[0:NH, :], AF.Exp),
             reads=[_h(ps_s[0].t)], writes=[_h(smt)], inc=s_act)
        v_e = p.do("scalar", lambda En: En.activation(sm(1), ps_s[2].t[0:NH, :], AF.Exp),
                   reads=[_h(ps_s[2].t)], writes=[_h(smt)], inc=s_act)
        p.wait("vector", s_act, v_e)
        v_sm = p.do("vector", lambda En: En.scalar_tensor_tensor(sm(2), sm(0), 1.0, sm(1), ALU.add, ALU.add),
                    reads=[_h(smt)], writes=[_h(smt)], inc=s_dve)
        SS[("v_sm", s)] = v_sm
        return v_e

    def seg_softmax_b(s):
        p.wait("scalar", s_dve, SS[("v_sm", s)])
        p.do("scalar", lambda En: En.activation(sm(3), sm(2), AF.Ln),
             reads=[_h(smt)], writes=[_h(smt)], inc=s_act)
        v_rr = p.do("scalar", lambda En: En.activation(sm(2), sm(3), AF.Exp, scale=-1.0),
                    reads=[_h(smt)], writes=[_h(smt)], inc=s_act)
        p.wait("vector", s_act, v_rr)
        p.do("vector", lambda En: En.tensor_mul(wp_(0), sm(0), sm(2)),
             reads=[_h(smt)], writes=[_h(wpl)], inc=s_dve)
        v_w = p.do("vector", lambda En: En.tensor_mul(wp_(2), sm(1), sm(2)),
                   reads=[_h(smt)], writes=[_h(wpl)], inc=s_dve)
        ps_s[0].consumed(s_dve, v_w)
        ps_s[2].consumed(s_dve, v_w)
        SS[("v_w", s)] = v_w

    def seg_wsum(s):
        par = s % 2
        p.wait("tensor", s_dve, SS[("v_w", s)])
        v_evac = SS[("evac", s)]
        wbsets = [(ps_s[0], ps_s[1], ps_s[2]), (ps_mm[0], ps_mm[1], ps_st)]
        v_mixc = {}
        for dc in range(NCH):
            setr = wbsets[dc % 2]
            v_mm = {}
            for o in range(3):
                setr[o].wait_writer(p)
                rsrc = sm(2) if o == 1 else wp_(o)
                v_mm[o] = p.do("tensor",
                               lambda En, ot=setr[o].t, l=bpk[:, dc * 128:(dc + 1) * 128], r=rsrc:
                               En.matmul(ot[:], l, r, start=True, stop=True),
                               inc=s_pe)
            vb = {}
            for o in range(3):
                sl = (dc % 2) * 3 + o
                if wbe_guard[sl] is not None:
                    p.wait("scalar", *wbe_guard[sl])
                p.wait("scalar", s_pe, v_mm[o])
                vb[o] = p.do("scalar",
                             lambda En, o_=wbe[:, sl * 512:(sl + 1) * 512], i_=setr[o].t:
                             En.copy(o_, i_[:]),
                             reads=[_h(setr[o].t)], writes=[_h(wbe)], inc=s_act)
                setr[o].consumed(s_act, vb[o])
            p.wait("vector", s_act, v_evac[(2, dc)])
            s0i = (dc % 2) * 2
            s1i = s0i + 1
            t0 = pslots[s0i]
            sc0 = pslots[s1i]
            for sl in (s0i, s1i):
                if pslot_guard[sl] is not None:
                    p.wait("vector", *pslot_guard[sl])
                    pslot_guard[sl] = None
            vbase = par * PB + dc * XFW
            e_ = lambda o: wbe[:, ((dc % 2) * 3 + o) * 512: ((dc % 2) * 3 + o) * 512 + 512]
            p.wait("vector", s_act, vb[1])
            p.do("vector", lambda En, o_=t0, w_=e_(0), vv=v_sb[:, vbase: vbase + 512]:
                 En.tensor_mul(o_, w_, vv),
                 reads=[_h(wbe), _h(v_sb)], writes=[pslot_t[s0i]], inc=s_dve)
            p.do("vector", lambda En, o_=sc0, w_=e_(1), vv=v_sb[:, vbase + 1: vbase + 513]:
                 En.tensor_mul(o_, w_, vv),
                 reads=[_h(wbe), _h(v_sb)], writes=[pslot_t[s1i]], inc=s_dve)
            p.do("vector", lambda En, o_=t0, a=t0, b=sc0: En.tensor_add(o_, a, b),
                 reads=[pslot_t[s0i], pslot_t[s1i]], writes=[pslot_t[s0i]], inc=s_dve)
            p.wait("vector", s_act, vb[2])
            p.do("vector", lambda En, o_=sc0, w_=e_(2), vv=v_sb[:, vbase + 2: vbase + 514]:
                 En.tensor_mul(o_, w_, vv),
                 reads=[_h(wbe), _h(v_sb)], writes=[pslot_t[s1i]], inc=s_dve)
            va = p.do("vector", lambda En, o_=mixc(s, dc), a=t0, b=sc0:
                      En.tensor_add(o_, a, b),
                      reads=[pslot_t[s0i], pslot_t[s1i]], writes=[_h(k_sb)], inc=s_dve)
            for o in range(3):
                wbe_guard[(dc % 2) * 3 + o] = (s_dve, va)
            v_mixc[dc] = va
        g_wsum[par] = va
        SS[("mixc", s)] = v_mixc

    def seg_wo(s):
        par = s % 2
        v_mixc = SS[("mixc", s)]
        for dc in range(NCH):
            role = ps_mm[dc % 2]
            role.wait_writer(p)
            for kc in range(NCH):
                p.wait("tensor", s_dve, v_mixc[kc])
                lhs = wot_sb[:, kc * 1024 + dc * 128: kc * 1024 + dc * 128 + 128]
                v = p.do("tensor",
                         lambda En, o=role.t, l=lhs, r=mixc(s, kc),
                         a=(kc == 0), b=(kc == NCH - 1):
                         En.matmul(o[:], l, r, start=a, stop=b),
                         inc=(s_pe if kc == NCH - 1 else None))
            g_wo[par] = v
            p.wait("vector", s_pe, v)
            va = p.do("vector",
                      lambda En, o=r1c(s, dc), i_=role.t,
                      bb=bo_sb[:, dc:dc + 1],
                      xres=x_fm[:, par * PB + dc * XFW + 1: par * PB + dc * XFW + 513]:
                      En.scalar_tensor_tensor(o, i_[:], bb, xres, ALU.add, ALU.add),
                      reads=[_h(role.t), _h(x_fm)], writes=[_h(q_sb)], inc=s_dve)
            role.consumed(s_dve, va)
        g_r1[par] = s_dve.n

    def seg_ln1_stats_mm(s):
        p.wait("tensor", s_dve, g_r1[s % 2])
        ps_s[1].wait_writer(p)
        ps_st.wait_writer(p)
        v2s = 0
        for dc in range(NCH):
            p.do("tensor",
                 lambda En, o=ps_s[1].t, r=r1c(s, dc), a=(dc == 0), b=(dc == NCH - 1):
                 En.matmul(o[0:1, :], onesv[:], r, start=a, stop=b),
                 inc=s_pe)
            slot = 4 + dc % 2
            sq = pslots[slot]
            if pslot_guard[slot] is not None:
                p.wait("scalar", *pslot_guard[slot])
                pslot_guard[slot] = None
            if dc == 0:
                p.wait("scalar", s_dve, g_r1[s % 2])
            vv = p.do("scalar",
                      lambda En, o=sq, r=r1c(s, dc): En.activation(o, r, AF.Square),
                      reads=[_h(q_sb)], writes=[pslot_t[slot]], inc=s_act)
            p.wait("tensor", s_act, vv)
            v2s = p.do("tensor",
                       lambda En, o=ps_st.t, r=sq, a=(dc == 0), b=(dc == NCH - 1):
                       En.matmul(o[0:1, :], onesv[:], r, start=a, stop=b),
                       inc=s_pe)
            pslot_guard[slot] = (s_pe, v2s)
        SS[("v2s", s)] = v2s

    def seg_ln1_stats_tail(s):
        p.wait("vector", s_pe, SS[("v2s", s)])
        p.wait("vector", s_act, g_lnact)
        p.wait("vector", s_pe, SS.get("v_mb", 0))
        p.do("vector", lambda En: En.tensor_scalar_mul(sn(0), ps_s[1].t[0:1, :], 1.0 / D),
             reads=[_h(ps_s[1].t)], writes=[_h(stn)], inc=s_dve)
        p.do("vector", lambda En: En.tensor_scalar_mul(sn(1), ps_st.t[0:1, :], 1.0 / D),
             reads=[_h(ps_st.t)], writes=[_h(stn)], inc=s_dve)
        p.do("vector", lambda En: En.tensor_mul(sn(2), sn(0), sn(0)),
             reads=[_h(stn)], writes=[_h(stn)], inc=s_dve)
        v_var = p.do("vector", lambda En: En.tensor_sub(sn(1), sn(1), sn(2)),
                     reads=[_h(stn)], writes=[_h(stn)], inc=s_dve)
        ps_s[1].consumed(s_dve, v_var)
        ps_st.consumed(s_dve, v_var)
        p.wait("scalar", s_dve, v_var)
        p.do("scalar", lambda En: En.activation(sn(2), sn(1), AF.Ln, bias=LN_EPS),
             reads=[_h(stn)], writes=[_h(stn)], inc=s_act)
        v_rs = p.do("scalar", lambda En: En.activation(sn(1), sn(2), AF.Exp, scale=-0.5),
                    reads=[_h(stn)], writes=[_h(stn)], inc=s_act)
        SS[("v_rs", s)] = v_rs
        return v_rs

    def seg_ln1_stats_tail_b(s):
        p.wait("vector", s_act, SS[("v_rs", s)])
        p.wait("vector", s_pe, SS.get("v_mb", 0))
        p.do("vector", lambda En: En.tensor_mul(sn(2), sn(0), sn(1)),
             reads=[_h(stn)], writes=[_h(stn)], inc=s_dve)
        p.do("vector", lambda En: En.tensor_copy(stb[:, 0:512], sn(1)),
             reads=[_h(stn)], writes=[_h(stb)], inc=s_dve)
        v_stb = p.do("vector", lambda En: En.tensor_copy(stb[:, 512:1024], sn(2)),
                     reads=[_h(stn)], writes=[_h(stb)], inc=s_dve)
        SS[("v_stb", s)] = v_stb

    def seg_ln1_tail(s):
        par = s % 2
        p.wait("tensor", s_dve, SS[("v_stb", s)])
        ps_s[1].wait_writer(p)
        p.do("tensor", lambda En: En.matmul(ps_s[1].t[:], onescol[0:1, :],
                                            stb[0:1, 0:512], start=True, stop=True),
             inc=s_pe)
        ps_st.wait_writer(p)
        v_mb = p.do("tensor", lambda En: En.matmul(ps_st.t[:], onescol[0:1, :],
                                                   stb[0:1, 512:1024], start=True, stop=True),
                    inc=s_pe)
        SS["v_mb"] = v_mb
        p.wait("scalar", s_pe, v_mb)
        sbp = pslots[2]
        mbp = pslots[3]
        for sl in (2, 3):
            if pslot_guard[sl] is not None:
                p.wait("scalar", *pslot_guard[sl])
                pslot_guard[sl] = None
        p.do("scalar", lambda En: En.copy(sbp, ps_s[1].t[:]),
             reads=[_h(ps_s[1].t)], writes=[pslot_t[2]], inc=s_act)
        v_bb = p.do("scalar", lambda En: En.copy(mbp, ps_st.t[:]),
                    reads=[_h(ps_st.t)], writes=[pslot_t[3]], inc=s_act)
        ps_s[1].consumed(s_act, v_bb)
        ps_st.consumed(s_act, v_bb)
        pslot_guard[2] = (s_act, v_bb)
        pslot_guard[3] = (s_act, v_bb)
        p.wait("vector", s_act, v_bb)
        if rstg_free[par] is not None:
            p.wait("vector", *rstg_free[par])
        for dc in range(NCH):
            t0n = pslots[dc % 2]
            p.do("vector", lambda En, o=t0n, r=r1c(s, dc):
                 En.tensor_mul(o, r, sbp),
                 reads=[_h(q_sb), pslot_t[2]], writes=[pslot_t[dc % 2]], inc=s_dve)
            p.do("vector",
                 lambda En, o=r1n_stg[:, par * NCH * SLAB + dc * SLAB: par * NCH * SLAB + (dc + 1) * SLAB],
                 a=t0n:
                 En.tensor_sub(o, a, mbp),
                 reads=[pslot_t[dc % 2], pslot_t[3]], writes=[_h(r1n_stg)], inc=s_dve)
        g_r1n[par] = s_dve.n
        p.wait("sync", s_hw, s_hw.n)
        p.wait("sync", s_dve, s_dve.n)
        p.do("sync",
             lambda En, o=r1n_hbm.rearrange("p (c t) -> p c t", t=T)[:, :, s * SLAB:(s + 1) * SLAB],
             i_=r1n_stg[:, par * NCH * SLAB:(par + 1) * NCH * SLAB].rearrange("p (c t) -> p c t", t=SLAB):
             En.dma_start(out=o, in_=i_),
             inc=s_hw, inc_by=16)
        rstg_free[par] = (s_hw, s_hw.n)

    seg_xt(0)
    seg_qkv(0)
    seg_xt(1)
    for s in range(NSLAB):
        seg_scores(s)
        if s >= 1:
            seg_ln1_stats_tail_b(s - 1)
        v_e = seg_softmax_a(s)
        g_exp = v_e
        if s >= 1:
            seg_ln1_tail(s - 1)
        seg_softmax_b(s)
        if s + 1 < NSLAB:
            seg_qkv(s + 1)
        seg_wsum(s)
        seg_wo(s)
        seg_ln1_stats_mm(s)
        if s + 2 < NSLAB:
            seg_xt(s + 2)
        v_rs = seg_ln1_stats_tail(s)
        g_lnact = v_rs
        g_mb = s_pe.n
    seg_ln1_stats_tail_b(NSLAB - 1)
    seg_ln1_tail(NSLAB - 1)

    p.wait("gpsimd", s_pe, s_pe.n)
    p.wait("gpsimd", s_dve, s_dve.n)
    p.wait("gpsimd", s_act, s_act.n)
    p.wait("gpsimd", s_hw, s_hw.n)
    for eng in ("tensor", "vector", "scalar", "sync"):
        p.wait(eng, s_pe, s_pe.n)
        p.wait(eng, s_dve, s_dve.n)
        p.wait(eng, s_act, s_act.n)
    p.flush()
    ph.close()
    pa.close()

    ph = ExitStack()
    psb = lambda name, shape, dt: ph.enter_context(nc.sbuf_tensor(name, shape, dt)).ap()
    pst = lambda name, shape, dt: ph.enter_context(nc.psum_tensor(name, shape, dt)).ap()
    r1n_ps = psb("r1n_ps", [128, 2 * NCH * PW], BF16)
    w1e = psb("w1e", [128, NCH * 1024], BF16)
    w2e = psb("w2e", [128, 2 * NCH * 1024], BF16)
    y_acc = psb("y_acc", [128, NCH * PW], F32)
    a_sb = psb("a_sb", [128, 2 * NCH * 512], BF16)
    t1u = psb("t1u", [128, 4 * 512], BF16)
    stn2 = psb("stn2", [1, 3 * 512], F32)
    stb2 = psb("stb2", [1, 1024], BF16)
    of_sb = psb("of", [128, 2 * 512], F32)
    otok = psb("otok", [128, 4 * 1024], F32)
    rtr = psb("rtr", [8, 2 * 512], F32)
    ln2scr = psb("ln2scr", [128, 2 * 512], BF16)
    ln2bc = psb("ln2bc", [128, 2 * 512], BF16)
    ln2mb = psb("ln2mb", [128, 2 * 512], BF16)
    ln2t = psb("ln2t", [128, 8 * 512], BF16)
    tpk = psb("tpk", [128, 32], F32)
    selp = psb("sel_sb", [E, E * 128], BF16)
    ps_gb = _PsumRole(pst("ps_gb", [128, 512], F32))
    ps_hh = [_PsumRole(pst("ps_h0", [128, 512], F32)),
             _PsumRole(pst("ps_h1", [128, 512], F32))]
    ps_yy = [_PsumRole(pst("ps_y0", [128, 512], F32)),
             _PsumRole(pst("ps_y1", [128, 512], F32))]
    ps_b = [_PsumRole(pst("ps_b0", [128, 512], F32)),
            _PsumRole(pst("ps_b1", [128, 512], F32)),
            _PsumRole(pst("ps_b2", [128, 512], F32))]
    sn2 = lambda i: stn2[:, i * 512:(i + 1) * 512]
    rt = lambda i: rtr[:, i * 512:(i + 1) * 512]
    tp3 = tpk.rearrange("p (g k) -> p g k", k=E)
    g3 = lambda a: a.rearrange("p (g o) -> p g o", o=1).broadcast_to([128, 4, E])
    t_m1, t_m2 = of_sb[:, 0:4], of_sb[:, 4:8]
    t_s, t_rs = of_sb[:, 8:12], of_sb[:, 12:16]
    t_mlt, t_pm = of_sb[:, 512:544], of_sb[:, 544:576]
    t_kp, t_kt = of_sb[:, 576:608], of_sb[:, 608:640]

    ldb()
    dma("gpsimd", selp[:], sel_h[:], s_ld, writes=[_h(selp)])
    v_sel = s_ld.n

    slice_ready = [0, 0]
    slice_guard = [None, None]
    h_done = {}
    r2_done = {}

    def load_slice(Hh_):
        rp = Hh_ % 2
        p.wait("sync", s_hw, s_hw.n)
        if slice_guard[rp] is not None:
            for w_ in slice_guard[rp]:
                p.wait("sync", *w_)
        dma("sync",
            r1n_ps[:, rp * NCH * PW:(rp + 1) * NCH * PW].rearrange("p (c t) -> p c t", t=PW),
            r1n_hbm.rearrange("p (c t) -> p c t", t=T)[:, :, Hh_ * PW:(Hh_ + 1) * PW],
            s_hw, writes=[_h(r1n_ps)])
        slice_ready[rp] = s_hw.n
        p.wait("sync", s_hw, s_hw.n)

    gates_ready = {}

    def seg_router(Hh_):
        rp = Hh_ % 2
        p.wait("tensor", s_hw, slice_ready[rp])
        for sl in range(PW // SLAB):
            ps_mm0 = ps_hh[0]
            ps_mm0.wait_writer(p)
            for kc in range(NCH):
                v = p.do("tensor",
                         lambda En, o=ps_mm0.t, l=rwt_sb[:, kc * E:(kc + 1) * E],
                         r=r1n_ps[:, rp * NCH * PW + kc * PW + sl * SLAB: rp * NCH * PW + kc * PW + (sl + 1) * SLAB],
                         a=(kc == 0), b=(kc == NCH - 1):
                         En.matmul(o[0:E, :], l, r, start=a, stop=b),
                         inc=(s_pe if kc == NCH - 1 else None))
            p.wait("scalar", s_pe, v)
            va = p.do("scalar", lambda En: En.activation(rt(0), ps_mm0.t[0:E, :], AF.Exp, scale=-1.0),
                      reads=[_h(ps_mm0.t)], writes=[_h(rtr)], inc=s_act)
            ps_mm0.consumed(s_act, va)
            p.do("scalar", lambda En: En.activation(rt(1), rt(0), AF.Ln, bias=1.0),
                 reads=[_h(rtr)], writes=[_h(rtr)], inc=s_act)
            v_p = p.do("scalar", lambda En: En.activation(rt(0), rt(1), AF.Exp, scale=-1.0),
                       reads=[_h(rtr)], writes=[_h(rtr)], inc=s_act)
            p.wait("tensor", s_act, v_p)
            ps_mm1 = ps_hh[1]
            ps_mm1.wait_writer(p)
            for g in range(4):
                v = p.do("tensor",
                         lambda En, o=ps_mm1.t, g_=g,
                         src=rtr[:, g * 128: (g + 1) * 128]:
                         En.transpose(o[:, g_ * 8:(g_ + 1) * 8], src, id32[0:E, 0:E]),
                         inc=(s_pe if g == 3 else None))
            p.wait("vector", s_pe, v)
            va = p.do("vector", lambda En: En.tensor_copy(tpk[:], ps_mm1.t[:, 0:32]),
                      reads=[_h(ps_mm1.t)], writes=[_h(tpk)], inc=s_dve)
            ps_mm1.consumed(s_dve, va)
            p.do("vector", lambda En: En.reduce_max(t_m1, tp3, axis=AX.X),
                 reads=[_h(tpk)], writes=[_h(of_sb)], inc=s_dve)
            p.do("vector", lambda En: En.tensor_tensor(t_mlt.rearrange("p (g k) -> p g k", k=E),
                                                       tp3, g3(t_m1), ALU.is_lt),
                 reads=[_h(tpk), _h(of_sb)], writes=[_h(of_sb)], inc=s_dve)
            p.do("vector", lambda En: En.tensor_mul(t_pm, tpk[:], t_mlt),
                 reads=[_h(tpk), _h(of_sb)], writes=[_h(of_sb)], inc=s_dve)
            p.do("vector", lambda En: En.reduce_max(t_m2, t_pm.rearrange("p (g k) -> p g k", k=E), axis=AX.X),
                 reads=[_h(of_sb)], writes=[_h(of_sb)], inc=s_dve)
            p.do("vector", lambda En: En.tensor_tensor(t_kp.rearrange("p (g k) -> p g k", k=E),
                                                       tp3, g3(t_m2), ALU.is_ge),
                 reads=[_h(tpk), _h(of_sb)], writes=[_h(of_sb)], inc=s_dve)
            p.do("vector", lambda En: En.tensor_mul(t_kt, tpk[:], t_kp),
                 reads=[_h(tpk), _h(of_sb)], writes=[_h(of_sb)], inc=s_dve)
            p.do("vector", lambda En: En.reduce_sum(t_s, t_kt.rearrange("p (g k) -> p g k", k=E), axis=AX.X),
                 reads=[_h(of_sb)], writes=[_h(of_sb)], inc=s_dve)
            p.do("vector", lambda En: En.tensor_scalar_add(t_s, t_s, 1e-20),
                 reads=[_h(of_sb)], writes=[_h(of_sb)], inc=s_dve)
            p.do("vector", lambda En: En.reciprocal(t_rs, t_s),
                 reads=[_h(of_sb)], writes=[_h(of_sb)], inc=s_dve)
            v_g = p.do("vector", lambda En: En.tensor_tensor(t_kt.rearrange("p (g k) -> p g k", k=E),
                                                             t_kt.rearrange("p (g k) -> p g k", k=E),
                                                             g3(t_rs), ALU.mult),
                       reads=[_h(of_sb)], writes=[_h(of_sb)], inc=s_dve)
            p.wait("tensor", s_dve, v_g)
            ps_b[1].wait_writer(p)
            for g in range(4):
                v = p.do("tensor",
                         lambda En, o=ps_b[1].t, g_=g, src=t_kt[:, g * 8:(g + 1) * 8]:
                         En.transpose(o[0:E, g_ * 128:(g_ + 1) * 128], src, id32[:]),
                         inc=(s_pe if g == 3 else None))
            p.wait("vector", s_pe, v)
            va = p.do("vector",
                      lambda En, o=gates[:, Hh_ * PW + sl * SLAB: Hh_ * PW + (sl + 1) * SLAB]:
                      En.tensor_copy(o, ps_b[1].t[0:E, 0:512]),
                      reads=[_h(ps_b[1].t)], writes=[_h(gates)], inc=s_dve)
            ps_b[1].consumed(s_dve, va)




    w_free = [None, None]
    w_free2 = [None, None]
    h_mm_done = {}
    a_free = [None, None]
    t1_free = [None, None]
    sqb_guard = [None, None]
    of_free = [None, None]
    otok_free = [None]
    n_store = 0
    ln2_freed = {}
    state = {"n_store": 0}

    def stream_w1(e):
        ldb()
        if w_free[0] is not None:
            p.wait("gpsimd", *w_free[0])
        dma("gpsimd", w1e.rearrange("p (c n) -> p c n", n=1024),
            w1.rearrange("(c p) n -> p c n", p=128)[:, :, e * 1024:(e + 1) * 1024],
            s_ld, writes=[_h(w1e)])
        return s_ld.n

    def stream_w2(e):
        ldb()
        sl = e % 2
        if w_free2[sl] is not None:
            p.wait("gpsimd", *w_free2[sl])
        dma("gpsimd",
            w2e[:, sl * NCH * 1024:(sl + 1) * NCH * 1024].rearrange("p (c n) -> p c n", n=1024),
            w2[e * 1024:(e + 1) * 1024, :].rearrange("(c p) n -> p c n", p=128),
            s_ld, writes=[_h(w2e)])
        return s_ld.n

    def emit_y(st_):
        apar, vb, v_w2e, tt, Hh_, e_ = st_
        w2b = (e_ % 2) * NCH * 1024
        p.wait("tensor", s_dve, vb)
        p.wait("tensor", s_ld, v_w2e)
        for ydc in range(NCH):
            role = ps_yy[ydc % 2]
            role.wait_writer(p)
            for wc in range(NCH):
                lhs = w2e[:, w2b + wc * 1024 + ydc * 128: w2b + wc * 1024 + ydc * 128 + 128]
                rhs = a_sb[:, apar * NCH * 512 + wc * 512: apar * NCH * 512 + (wc + 1) * 512]
                v = p.do("tensor",
                         lambda En, o=role.t, l=lhs, r=rhs,
                         a=(wc == 0), b=(wc == NCH - 1):
                         En.matmul(o[:], l, r, start=a, stop=b),
                         inc=(s_pe if wc == NCH - 1 else None))
            p.wait("vector", s_pe, v)
            if e_ == 0 and (tt, ydc) in ln2_freed:
                dv, av = ln2_freed[(tt, ydc)]
                p.wait("vector", s_dve, dv)
                p.wait("vector", s_act, av)
            ydst = y_acc[:, ydc * PW + tt * TT: ydc * PW + tt * TT + TT]
            if e_ == 0:
                va = p.do("vector", lambda En, o=ydst, i_=role.t: En.tensor_copy(o, i_[:]),
                          reads=[_h(role.t)], writes=[_h(y_acc)], inc=s_dve)
            else:
                va = p.do("vector", lambda En, o=ydst, i_=role.t: En.tensor_add(o, o, i_[:]),
                          reads=[_h(role.t), _h(y_acc)], writes=[_h(y_acc)], inc=s_dve)
            role.consumed(s_dve, va)
        a_free[apar] = (s_pe, v)

    ln2st = {}
    ln2mb_guard = [0]

    a_r2v = {}

    def seg_ln2_a_r2(Hh_, tt):
        rp_ = Hh_ % 2
        for dc in range(NCH):
            ydst = y_acc[:, dc * PW + tt * TT: dc * PW + tt * TT + TT]
            a_r2v[(tt, dc)] = p.do("vector",
                lambda En, o=ydst,
                b=r1n_ps[:, rp_ * NCH * PW + dc * PW + tt * TT: rp_ * NCH * PW + dc * PW + (tt + 1) * TT]:
                En.tensor_add(o, o, b),
                reads=[_h(y_acc), _h(r1n_ps)], writes=[_h(y_acc)], inc=s_dve)
        r2_done[Hh_] = s_dve.n

    def seg_ln2_a(Hh_, tt):
        ps_b[1].wait_writer(p)
        ps_b[2].wait_writer(p)
        v2s = 0
        vc = 0
        for dc in range(NCH):
            ydst = y_acc[:, dc * PW + tt * TT: dc * PW + tt * TT + TT]
            rb = ln2scr[:, (dc % 2) * 512:(dc % 2) * 512 + 512]
            p.wait("scalar", s_dve, a_r2v[(tt, dc)])
            if sqb_guard[dc % 2] is not None:
                p.wait("scalar", *sqb_guard[dc % 2])
            vc = p.do("scalar", lambda En, o=rb, i_=ydst: En.copy(o, i_),
                      reads=[_h(y_acc)], writes=[_h(ln2scr)], inc=s_act)
            p.wait("tensor", s_act, vc)
            p.do("tensor",
                 lambda En, o=ps_b[1].t, r=rb, a=(dc == 0), b=(dc == NCH - 1):
                 En.matmul(o[0:1, :], onesv[:], r, start=a, stop=b),
                 inc=s_pe)
            sq2 = ln2bc[:, (dc % 2) * 512:(dc % 2) * 512 + 512]
            vs = p.do("scalar", lambda En, o=sq2, i_=rb: En.activation(o, i_, AF.Square),
                      reads=[_h(ln2scr)], writes=[_h(ln2bc)], inc=s_act)
            p.wait("tensor", s_act, vs)
            v2s = p.do("tensor",
                       lambda En, o=ps_b[2].t, r=sq2, a=(dc == 0), b=(dc == NCH - 1):
                       En.matmul(o[0:1, :], onesv[:], r, start=a, stop=b),
                       inc=s_pe)
            sqb_guard[dc % 2] = (s_pe, v2s)
        ln2st["v2s"] = v2s
        ln2st["v_casts"] = vc

    def seg_ln2_b_chain(Hh_, tt):
        p.wait("vector", s_pe, ln2st["v2s"])
        p.wait("vector", s_act, ln2st.get("v_stb2", 0))
        p.do("vector", lambda En: En.tensor_scalar_mul(sn2(0), ps_b[1].t[0:1, :], 1.0 / D),
             reads=[_h(ps_b[1].t)], writes=[_h(stn2)], inc=s_dve)
        p.do("vector", lambda En: En.tensor_scalar_mul(sn2(1), ps_b[2].t[0:1, :], 1.0 / D),
             reads=[_h(ps_b[2].t)], writes=[_h(stn2)], inc=s_dve)
        p.do("vector", lambda En: En.tensor_mul(sn2(2), sn2(0), sn2(0)),
             reads=[_h(stn2)], writes=[_h(stn2)], inc=s_dve)
        v_var = p.do("vector", lambda En: En.tensor_sub(sn2(1), sn2(1), sn2(2)),
                     reads=[_h(stn2)], writes=[_h(stn2)], inc=s_dve)
        ps_b[1].consumed(s_dve, v_var)
        ps_b[2].consumed(s_dve, v_var)
        p.wait("scalar", s_dve, v_var)
        p.do("scalar", lambda En: En.activation(sn2(2), sn2(1), AF.Ln, bias=LN_EPS),
             reads=[_h(stn2)], writes=[_h(stn2)], inc=s_act)
        v_rs = p.do("scalar", lambda En: En.activation(sn2(1), sn2(2), AF.Exp, scale=-0.5),
                    reads=[_h(stn2)], writes=[_h(stn2)], inc=s_act)
        p.wait("scalar", s_pe, ln2st.get("v_bc_pe", 0))
        p.do("scalar", lambda En: En.copy(stb2[:, 0:512], sn2(0)),
             reads=[_h(stn2)], writes=[_h(stb2)], inc=s_act)
        v_stb2 = p.do("scalar", lambda En: En.copy(stb2[:, 512:1024], sn2(1)),
                      reads=[_h(stn2)], writes=[_h(stb2)], inc=s_act)
        ln2st["v_stb2"] = v_stb2

    def seg_ln2_b_bcast(Hh_, tt):
        p.wait("tensor", s_act, ln2st["v_stb2"])
        ps_b[1].wait_writer(p)
        p.do("tensor", lambda En: En.matmul(ps_b[1].t[:], onescol[0:1, :],
                                            stb2[0:1, 0:512], start=True, stop=True),
             inc=s_pe)
        ps_b[2].wait_writer(p)
        v = p.do("tensor", lambda En: En.matmul(ps_b[2].t[:], onescol[0:1, :],
                                                stb2[0:1, 512:1024], start=True, stop=True),
                 inc=s_pe)
        ln2st["v_bc_pe"] = v
        p.wait("scalar", s_pe, v)
        p.wait("scalar", s_dve, ln2mb_guard[0])
        p.do("scalar", lambda En: En.copy(ln2mb[:, 0:512], ps_b[1].t[:]),
             reads=[_h(ps_b[1].t)], writes=[_h(ln2mb)], inc=s_act)
        va = p.do("scalar", lambda En: En.copy(ln2mb[:, 512:1024], ps_b[2].t[:]),
                  reads=[_h(ps_b[2].t)], writes=[_h(ln2mb)], inc=s_act)
        ps_b[1].consumed(s_act, va)
        ps_b[2].consumed(s_act, va)
        ln2st["v_mb"] = va

    def seg_ln2_c_subs(Hh_, tt):
        p.wait("vector", s_act, ln2st["v_mb"])
        for dc in range(NCH):
            ysrc = y_acc[:, dc * PW + tt * TT: dc * PW + tt * TT + TT]
            t_ = ln2t[:, dc * 512:(dc + 1) * 512]
            v_sub = p.do("vector",
                         lambda En, o=t_, a=ysrc: En.tensor_sub(o, a, ln2mb[:, 0:512]),
                         reads=[_h(y_acc), _h(ln2mb)], writes=[_h(ln2t)], inc=s_dve)
            ln2_freed[(tt, dc)] = (v_sub, ln2st["v_casts"])

    def seg_ln2_c_muls(Hh_, tt):
        tok0 = Hh_ * PW + tt * TT
        if otok_free[0] is not None:
            p.wait("vector", *otok_free[0])
        for dc in range(NCH):
            p.do("vector",
                 lambda En, o=otok[:, dc * 512:(dc + 1) * 512],
                 a=ln2t[:, dc * 512:(dc + 1) * 512]:
                 En.tensor_mul(o, a, ln2mb[:, 512:1024]),
                 reads=[_h(ln2t), _h(ln2mb)], writes=[_h(otok)], inc=s_dve)
        v_otok = s_dve.n
        ln2mb_guard[0] = v_otok
        p.wait("sync", s_dve, v_otok)
        p.do("sync",
             lambda En,
             o=out_h.rearrange("p (c t) -> p c t", t=T)[:, :, tok0: tok0 + TT],
             i_=otok[:, 0:NCH * 512].rearrange("p (c t) -> p c t", t=512):
             En.dma_start(out=o, in_=i_),
             inc=s_hw, inc_by=16)
        state["n_store"] += 1
        otok_free[0] = (s_hw, s_hw.n)


    p.wait("tensor", s_ld, v_sel)
    ln2_pending = []
    load_slice(0)
    seg_router(0)
    gates_ready[0] = s_dve.n
    for Hh in range(NPASS):
        rp = Hh % 2
        v_w1 = stream_w1(0)
        if Hh == 0:
            v_w2 = stream_w2(0)
            pending = None
        v_h_last = 0
        for e in range(E):
            for tt in range(NTT):
                tok0 = Hh * PW + tt * TT
                apar = (e * NTT + tt) % 2
                gbank = ps_gb if apar == 0 else ps_b[0]
                gbank.wait_writer(p)
                if e == 0 and tt == 0:
                    p.wait("tensor", s_dve, gates_ready[Hh])
                v_gb = p.do("tensor",
                            lambda En, o=gbank.t, l=selp[:, e * 128:(e + 1) * 128],
                            r=gates[:, tok0: tok0 + TT]:
                            En.matmul(o[:], l, r, start=True, stop=True), inc=s_pe)
                p.wait("tensor", s_ld, v_w1)
                if a_free[apar] is not None:
                    p.wait("vector", *a_free[apar])
                for hdc in range(NCH):
                    role = ps_hh[hdc % 2]
                    role.wait_writer(p)
                    for kc in range(NCH):
                        lhs = w1e[:, kc * 1024 + hdc * 128: kc * 1024 + hdc * 128 + 128]
                        rhs = r1n_ps[:, rp * NCH * PW + kc * PW + tt * TT: rp * NCH * PW + kc * PW + (tt + 1) * TT]
                        v = p.do("tensor",
                                 lambda En, o=role.t, l=lhs, r=rhs,
                                 a=(kc == 0), b=(kc == NCH - 1):
                                 En.matmul(o[:], l, r, start=a, stop=b),
                                 inc=(s_pe if kc == NCH - 1 else None))
                    p.wait("scalar", s_pe, v)
                    t1 = t1u[:, (hdc % 2) * 512:(hdc % 2) * 512 + 512]
                    u = t1u[:, 1024 + (hdc % 2) * 512: 1024 + (hdc % 2) * 512 + 512]
                    if t1_free[hdc % 2] is not None:
                        p.wait("scalar", *t1_free[hdc % 2])
                    va = p.do("scalar", lambda En, o=t1, i_=role.t: En.activation(o, i_[:], AF.Relu),
                              reads=[_h(role.t)], writes=[_h(t1u)], inc=s_act)
                    role.consumed(s_act, va)
                    p.wait("vector", s_act, va)
                    p.wait("vector", s_pe, v_gb)
                    p.do("vector", lambda En, o=u, a=t1, gb_=gbank.t: En.tensor_mul(o, a, gb_[:]),
                         reads=[_h(t1u), _h(gbank.t)], writes=[_h(t1u)], inc=s_dve)
                    vb = p.do("vector",
                              lambda En, o=a_sb[:, apar * NCH * 512 + hdc * 512: apar * NCH * 512 + (hdc + 1) * 512],
                              a=u, b=t1: En.tensor_mul(o, a, b),
                              reads=[_h(t1u)], writes=[_h(a_sb)], inc=s_dve)
                    t1_free[hdc % 2] = (s_dve, vb)
                gbank.consumed(s_dve, vb)
                if tt == NTT - 1:
                    v_h_last = s_pe.n
                prev = pending
                boundary = (prev is not None and prev[5] == E - 1
                            and prev[3] == NTT - 1)
                pre, post = ln2_pending.pop(0) if ln2_pending else ((), ())
                if boundary:
                    emit_y(prev)
                    w_free2[prev[5] % 2] = (s_pe, s_pe.n)
                    v_w2 = stream_w2(0)
                for f in pre:
                    f()
                if prev is not None and not boundary:
                    emit_y(prev)
                    w_free2[prev[5] % 2] = (s_pe, s_pe.n)
                    if prev[5] == e - 1 and prev[3] == NTT - 1 and e > 0:
                        v_w2 = stream_w2(e)
                for f in post:
                    f()
                if Hh == NPASS - 1 and e == E - 1 and tt == NTT - 1 and NTT >= 2:
                    seg_ln2_a_r2(Hh, 0)
                    seg_ln2_a(Hh, 0)
                    seg_ln2_b_chain(Hh, 0)
                    seg_ln2_b_bcast(Hh, 0)
                    seg_ln2_c_subs(Hh, 0)
                    seg_ln2_c_muls(Hh, 0)
                pending = (apar, vb, v_w2, tt, Hh, e)
            if e == 1 and Hh + 1 < NPASS:
                if Hh >= 1:
                    slice_guard[(Hh + 1) % 2] = [(s_pe, h_done[Hh - 1]),
                                                 (s_dve, r2_done[Hh - 1])]
                load_slice(Hh + 1)
            if e == 3 and Hh + 1 < NPASS:
                seg_router(Hh + 1)
                gates_ready[Hh + 1] = s_dve.n
            if e + 1 < E:
                w_free[0] = (s_pe, v_h_last)
                v_w1 = stream_w1(e + 1)
        h_done[Hh] = v_h_last
        w_free[0] = (s_pe, v_h_last)
        mk = lambda fn, tt: (lambda Hh_=Hh, tt_=tt: fn(Hh_, tt_))
        if Hh == NPASS - 1 and NTT >= 2:
            ln2_pending = []
            for t_ in range(1, NTT):
                ln2_pending.append(([mk(seg_ln2_a_r2, t_), mk(seg_ln2_a, t_),
                                     mk(seg_ln2_b_chain, t_)], []))
                ln2_pending.append(([mk(seg_ln2_b_bcast, t_), mk(seg_ln2_c_subs, t_)],
                                    [mk(seg_ln2_c_muls, t_)]))
        else:
            ln2_pending = [([mk(seg_ln2_a_r2, t_) for t_ in range(NTT)]
                            + [mk(seg_ln2_a, 0), mk(seg_ln2_b_chain, 0)], [])]
            for t_ in range(1, NTT):
                ln2_pending.append(([mk(seg_ln2_b_bcast, t_ - 1), mk(seg_ln2_c_subs, t_ - 1)],
                                    [mk(seg_ln2_c_muls, t_ - 1),
                                     mk(seg_ln2_a, t_), mk(seg_ln2_b_chain, t_)]))
            ln2_pending.append(([mk(seg_ln2_b_bcast, NTT - 1), mk(seg_ln2_c_subs, NTT - 1)],
                                [mk(seg_ln2_c_muls, NTT - 1)]))
    emit_y(pending)
    pending = None
    for pre, post in ln2_pending:
        for f in pre:
            f()
        for f in post:
            f()

    p.wait("sync", s_hw, s_hw.n)
    p.flush()
    ph.close()
    st.close()
    return nc


def _host_consts():
    spk = np.zeros((128, NCH * NH), dtype=np.float32)
    for c in range(NCH):
        for pp in range(128):
            spk[pp, c * NH + 2 * c + (1 if pp >= 64 else 0)] = 1.0
    bpk = np.zeros((NH, NCH * 128), dtype=np.float32)
    for c in range(NCH):
        for i in range(128):
            bpk[2 * c + (1 if i >= 64 else 0), c * 128 + i] = 1.0
    sel = np.zeros((E, E * 128), dtype=np.float32)
    for e in range(E):
        sel[e, e * 128:(e + 1) * 128] = 1.0
    ones = np.ones((1, 128), dtype=np.float32)
    id128 = np.eye(128, dtype=np.float32)
    return dict(spk=spk, spkn=-spk, bpk=bpk, selpk=sel, onespk=ones,
                onesv128=np.ones((128, 1), dtype=np.float32), id128=id128)


def make_in_maps(x, Wp, bp, Wq, bq, Wk, bk, Wv, bv, Wo, bo, router_w, w1, w2,
                 n_cores, T):
    B, S, Dm = x.shape
    per_seq = n_cores // B
    consts = _host_consts()
    c32 = lambda a: np.ascontiguousarray(a, dtype=np.float32)
    fm = lambda b: c32(np.reshape(np.asarray(b), (NCH, 128)).T)
    rwt_fm = c32(np.transpose(router_w).reshape(NCH, 128, E).transpose(1, 0, 2).reshape(128, NCH * E))
    shared = dict(
        wp=c32(Wp), wqt=c32(np.transpose(Wq)), wkt=c32(np.transpose(Wk)),
        wvt=c32(np.transpose(Wv)), wot=c32(np.transpose(Wo)),
        bp=fm(bp), bq=fm(bq), bk=fm(bk), bv=fm(bv), bo=fm(bo),
        rwt=rwt_fm, w1=c32(w1), w2=c32(w2), **consts)
    in_maps = []
    for c in range(n_cores):
        b = c // per_seq
        h = c % per_seq
        a0 = h * T
        lo = (a0 - 1) % S
        hi = (a0 + T) % S
        xsh = np.concatenate([x[b, lo:lo + 1, :], x[b, a0:a0 + T, :],
                              x[b, hi:hi + 1, :]], axis=0)
        m = dict(shared)
        m["xs"] = c32(xsh)
        in_maps.append(m)
    return in_maps


def kernel(L0_attn_x, L0_attn_Wp, L0_attn_bp, L0_attn_Wq, L0_attn_bq,
           L0_attn_Wk, L0_attn_bk, L0_attn_Wv, L0_attn_bv, L0_attn_Wo,
           L0_attn_bo, L0_ffn_router_w, L0_ffn_w1, L0_ffn_w2):
    x = np.asarray(L0_attn_x)
    B, S, Dm = x.shape
    n_cores = 8
    T = B * S // n_cores
    in_maps = make_in_maps(x, L0_attn_Wp, L0_attn_bp, L0_attn_Wq, L0_attn_bq,
                           L0_attn_Wk, L0_attn_bk, L0_attn_Wv, L0_attn_bv,
                           L0_attn_Wo, L0_attn_bo, L0_ffn_router_w, L0_ffn_w1,
                           L0_ffn_w2, n_cores, T)
    nc = build_program(T)
    res = run_bass_kernel_spmd(nc, in_maps, core_ids=list(range(n_cores)))
    shard0 = np.asarray(res.results[0]["out"])
    out = np.empty((B, S, Dm), dtype=shard0.dtype)
    per_seq = n_cores // B
    for c in range(n_cores):
        b = c // per_seq
        h = c % per_seq
        sh = np.asarray(res.results[c]["out"]).reshape(128, NCH, T)
        out[b, h * T:(h + 1) * T, :] = sh.transpose(2, 1, 0).reshape(T, Dm)
    return out
